# version 05
# baseline (speedup 1.0000x reference)
import sys
try:
    import concourse.bass
except ImportError:
    sys.path.insert(0, "/opt/trn_rl_repo")

"""Megakernel builder for mk_sandwich_d1 (Mamba-S6 + dense-MoE sandwich).

Sharding (8 cores = one TRN2 chip):
  Mamba : core c -> batch b=c//4, d_inner quarter q=c%4 (512 channels),
          full sequence (scan entirely on-core). x_dbl partials AllReduced
          per 1024-token block across the 4 cores of the batch; out_proj
          partials ReduceScattered to the FFN token owners.
  FFN   : core c -> 2048 tokens (quarter q of batch b); dense experts with
          top-3 routed weights; the pre-FFN LN is folded into the matmuls
          (rank-1 mean fixup; rstd applied after relu; biases pre-divided
          by rstd via a std-row bias matmul).
"""
import contextlib
import numpy as np
import ml_dtypes

import concourse.bass as bass
import concourse.mybir as mybir

F32 = mybir.dt.float32
BF16 = mybir.dt.bfloat16
AF = mybir.ActivationFunctionType
ALU = mybir.AluOpType
AX = mybir.AxisListType
bf = ml_dtypes.bfloat16

B, L, D = 2, 8192, 1024
DI, DS, DC, DTR = 2048, 16, 4, 64
E, H = 15, 1024
NCORE = 8
CH = DI // 4
TOK = L // 4
TBS = 1024
NTB = L // TBS
EPS = 1e-5
GROUPS = [[0, 1, 2, 3], [4, 5, 6, 7]]
ENGS = ("pe", "act", "dve", "pool", "sp")


class Sync:

    def __init__(self, nc, es):
        self.nc = nc
        self.eng = {}
        self.insts = {k: [] for k in ENGS}
        self.incs = {k: [] for k in ENGS}
        self.sem = {k: es.enter_context(nc.semaphore(f"sem_{k}")) for k in ENGS}
        self.clock = {a: {b: -1 for b in ENGS} for a in ENGS}
        self.dma_sems = {}
        self.dma_cnt = {}
        self.dma_clock = {a: {} for a in ENGS}
        self.cc_clock = {}
        self._cc = {}
        self.buf = {}
        self.es = es

    def set_queues(self, **eng_objs):
        self.eng = eng_objs

    def _dma_sem(self, tag):
        if tag not in self.dma_sems:
            self.dma_sems[tag] = self.es.enter_context(
                self.nc.semaphore(f"dma_{tag}"))
            self.dma_cnt[tag] = 0
        return self.dma_sems[tag]

    def _ensure_inc(self, En, i):
        incs = self.incs[En]
        for pos, k in enumerate(incs):
            if k >= i:
                return self.sem[En], pos + 1
        k = len(self.insts[En]) - 1
        while k >= 0 and (self.insts[En][k]["fn"] is None
                          or self.insts[En][k].get("dma")):
            k -= 1
        assert k >= i >= 0, f"dep on future inst {En}:{i} (have {k})"
        self.insts[En][k]["incs"].append((self.sem[En], 1))
        incs.append(k)
        return self.sem[En], len(incs)

    def _dep(self, eng, w, waits, dist_guard=2):
        kind, En, i = w
        if kind == "cc":
            ck = self.cc_clock.setdefault(eng, set())
            if En in ck:
                return
            waits.append((self._cc[En], 1))
            ck.add(En)
            return
        if kind == "dma":
            cnt = self.dma_cnt[En]
            if self.dma_clock[eng].get(En, 0) >= cnt:
                return
            waits.append((self._dma_sem(En), 16 * cnt))
            self.dma_clock[eng][En] = cnt
            return
        if En == eng:
            idx = len(self.insts[eng])
            if eng == "pe" or idx - i > dist_guard:
                return
        elif self.clock[eng][En] >= i:
            return
        sem, val = self._ensure_inc(En, i)
        waits.append((sem, val))
        if En != eng:
            self.clock[eng][En] = self.incs[En][val - 1]

    def _gather(self, eng, reads, writes, scalar_reads):
        waits = []
        for key in reads:
            ent = self.buf.get(key)
            if ent and ent["w"]:
                self._dep(eng, ent["w"], waits)
        for key in scalar_reads:
            ent = self.buf.get(key)
            if ent and ent["w"]:
                self._dep(eng, ent["w"], waits, dist_guard=4)
        for key in writes:
            ent = self.buf.get(key)
            if ent:
                if ent["w"]:
                    self._dep(eng, ent["w"], waits)
                for r in ent["r"]:
                    self._dep(eng, r, waits)
        ded = {}
        for sem, val in waits:
            k = id(sem)
            if k not in ded or ded[k][1] < val:
                ded[k] = (sem, val)
        return list(ded.values())

    def _record(self, who, reads, writes, scalar_reads=()):
        for key in writes:
            self.buf[key] = {"w": who, "r": []}
        for key in list(reads) + list(scalar_reads):
            ent = self.buf.setdefault(key, {"w": None, "r": []})
            ent["r"].append(who)

    def op(self, eng, fn, reads=(), writes=(), scalar_reads=()):
        waits = self._gather(eng, reads, writes, scalar_reads)
        rec = {"waits": waits, "fn": fn, "incs": []}
        idx = len(self.insts[eng])
        self.insts[eng].append(rec)
        self._record(("eng", eng, idx), reads, writes, scalar_reads)
        return rec

    def dma(self, queue, tag, out, in_, reads=(), writes=()):
        waits = self._gather(queue, reads, writes, ())
        sem = self._dma_sem(tag)
        rec = {"waits": waits,
               "fn": (lambda q, out=out, in_=in_: q.dma_start(out=out, in_=in_)),
               "incs": [(sem, 16)], "dma": True}
        self.insts[queue].append(rec)
        self.dma_cnt[tag] += 1
        self._record(("dma", tag, self.dma_cnt[tag]), reads, writes)
        return rec

    def collective(self, fn, tag, reads=(), writes=()):
        waits = self._gather("pool", reads, writes, ())
        sem = self.es.enter_context(self.nc.semaphore(f"cc_{tag}"))
        self._cc[tag] = sem
        rec = {"waits": waits, "fn": fn, "incs": [(sem, None)], "dma": True}
        self.insts["pool"].append(rec)
        self._record(("cc", tag, 1), reads, writes)
        return rec

    def barrier(self):
        for En in ENGS:
            eng_insts = [i for i, r in enumerate(self.insts[En])
                         if r["fn"] is not None and not r.get("dma")]
            if eng_insts:
                self._ensure_inc(En, eng_insts[-1])
        for eng in ENGS:
            waits = []
            for En in ENGS:
                if En == eng:
                    continue
                if self.incs[En]:
                    waits.append((self.sem[En], len(self.incs[En])))
                    self.clock[eng][En] = self.incs[En][-1]
            for tag, cnt in list(self.dma_cnt.items()):
                if cnt > 0 and self.dma_clock[eng].get(tag, 0) < cnt:
                    waits.append((self.dma_sems[tag], 16 * cnt))
                    self.dma_clock[eng][tag] = cnt
            for tag in self._cc:
                ck = self.cc_clock.setdefault(eng, set())
                if tag not in ck:
                    waits.append((self._cc[tag], 1))
                    ck.add(tag)
            self.insts[eng].append({"waits": waits, "fn": None, "incs": []})

    def replay(self, block):
        def runner(eng):
            def body(q):
                for rec in self.insts[eng]:
                    for sem, val in rec["waits"]:
                        q.wait_ge(sem, val)
                    if rec["fn"] is None:
                        continue
                    inst = rec["fn"](q)
                    for sem, amt in rec["incs"]:
                        if amt is None:
                            inst.then_inc(sem)
                        else:
                            inst.then_inc(sem, amt)
            return body
        block.tensor(runner("pe"))
        block.scalar(runner("act"))
        block.vector(runner("dve"))
        block.gpsimd(runner("pool"))
        block.sync(runner("sp"))


class Arena:
    def __init__(self, nc, es, name, dt, cols):
        self.t = es.enter_context(nc.sbuf_tensor(name, [128, cols], dt))
        self.off = 0
        self.cols = cols
        self.marks = []

    def alloc(self, cols):
        off = self.off
        self.off += cols
        assert self.off <= self.cols, f"arena overflow {self.off}>{self.cols}"
        return self.t[0:128, off:off + cols]

    def push(self):
        self.marks.append(self.off)

    def pop(self):
        self.off = self.marks.pop()


def host_prep(inputs):
    h = np.asarray(inputs["L0_attn_hidden_states"])
    W_in = np.asarray(inputs["L0_attn_W_in"])
    C = np.ascontiguousarray
    sel16 = np.zeros((16, 16 * 128), dtype=bf)
    for n in range(16):
        sel16[n, n * 128:(n + 1) * 128] = 1.0
    id128 = np.eye(128, dtype=np.float32)
    ones_row = np.ones((1, 512), dtype=bf)
    hT = [C(h[b].T) for b in range(B)]
    maps = []
    for c in range(NCORE):
        b, q = c // 4, c % 4
        chs = np.arange(CH * q, CH * (q + 1))
        Wsl = np.concatenate([W_in[2 * chs], W_in[2 * chs + 1]], axis=0)
        m = {
            "hT": hT[b],
            "hres": C(h[b, TOK * q:TOK * (q + 1), :].T),
            "w_inT": C(Wsl.T),
            "w_outT": C(np.asarray(inputs["L0_attn_W_out"])[:, chs].T),
            "xpwT": C(np.asarray(inputs["L0_attn_x_proj_w"])[:, chs].T),
            "dtwT": C(np.asarray(inputs["L0_attn_dt_w"])[chs, :].T),
            "alog": C(np.asarray(inputs["L0_attn_A_log"])[chs]),
            "convw": C(np.asarray(inputs["L0_attn_conv_w"])[chs]),
            "convb": C(np.asarray(inputs["L0_attn_conv_b"])[chs]),
            "dtb": C(np.asarray(inputs["L0_attn_dt_b"])[chs]),
            "dvec": C(np.asarray(inputs["L0_attn_D"])[chs]),
            "rw": np.asarray(inputs["L0_ffn_router_W"]),
            "rb": np.asarray(inputs["L0_ffn_router_b"]).reshape(1, E),
            "sw1": np.asarray(inputs["L0_ffn_sW1"]),
            "sw2": np.asarray(inputs["L0_ffn_sW2"]),
            "ew1": np.asarray(inputs["L0_ffn_eW1"]),
            "ew2": np.asarray(inputs["L0_ffn_eW2"]),
            "sb1r": np.asarray(inputs["L0_ffn_sb1"]).reshape(1, H),
            "sb2r": np.asarray(inputs["L0_ffn_sb2"]).reshape(1, D),
            "eb1r": np.asarray(inputs["L0_ffn_eb1"]).reshape(1, E * H),
            "eb2r": np.asarray(inputs["L0_ffn_eb2"]).reshape(1, E * D),
            "sel16": sel16, "id128": id128, "ones_row": ones_row,
        }
        maps.append(m)
    return maps


def declare_io(nc):
    d = {}
    def di(name, shape, dt=F32):
        d[name] = nc.dram_tensor(name, list(shape), dt, kind="ExternalInput").ap()
    di("hT", (D, L)); di("hres", (D, TOK))
    di("w_inT", (D, 2 * CH)); di("w_outT", (CH, D))
    di("xpwT", (CH, 96)); di("dtwT", (DTR, CH))
    di("alog", (CH, DS)); di("convw", (CH, DC))
    di("convb", (CH,)); di("dtb", (CH,)); di("dvec", (CH,))
    di("rw", (D, E)); di("rb", (1, E))
    di("sw1", (D, H)); di("sw2", (H, D))
    di("ew1", (E, D, H)); di("ew2", (E, H, D))
    di("sb1r", (1, H)); di("sb2r", (1, D))
    di("eb1r", (1, E * H)); di("eb2r", (1, E * D))
    di("sel16", (16, 16 * 128), BF16)
    di("id128", (128, 128))
    di("ones_row", (1, 512), BF16)
    d["out"] = nc.dram_tensor("out", [TOK, D], F32, kind="ExternalOutput").ap()
    for tb in range(NTB):
        d[f"ccx_in{tb}"] = nc.dram_tensor(f"ccx_in{tb}", [96, TBS], BF16, kind="Internal").ap()
        d[f"ccx_out{tb}"] = nc.dram_tensor(f"ccx_out{tb}", [96, TBS], BF16, kind="Internal").ap()
    d["ccm_in"] = nc.dram_tensor("ccm_in", [NTB * D, TBS], BF16, kind="Internal").ap()
    d["wbf"] = nc.dram_tensor("wbf", [32, D, H], BF16, kind="Internal").ap()
    d["ebbf"] = nc.dram_tensor("ebbf", [1, 2 * E * H], BF16, kind="Internal").ap()
    d["ccm_out"] = nc.dram_tensor("ccm_out", [2 * D, TBS], BF16, kind="Internal").ap()
    return d


def build_program(upto="full", dbg_names=()):
    nc = bass.Bass("TRN2", target_bir_lowering=False)
    io = declare_io(nc)
    es = contextlib.ExitStack()
    s = Sync(nc, es)

    dbg = []
    for i in range(len(dbg_names)):
        dbg.append(nc.dram_tensor(f"dbg{i}", [128, 16 * 1024], F32, kind="ExternalOutput").ap())

    sb_ = lambda n, shape, dt: es.enter_context(nc.sbuf_tensor(n, shape, dt))

    xpwT = sb_("sb_xpwT", [128, 4 * 96], BF16)
    dtwT = sb_("sb_dtwT", [64, 4 * 128], BF16)
    A_sb = sb_("A_sb", [128, 4 * 16], F32)
    convw = sb_("sb_convw", [128, 4 * 4], F32)
    convb = sb_("sb_convb", [128, 4], F32)
    dtb = sb_("sb_dtb", [128, 4], F32)
    dvec = sb_("sb_dvec", [128, 4], F32)
    ndtb = sb_("sb_ndtb", [128, 4], F32)
    rw_sb = sb_("rw_sb", [128, 8 * E], BF16)
    sel16 = sb_("sb_sel16", [16, 16 * 128], BF16)
    id128 = sb_("sb_id128", [128, 128], F32)
    id128b = sb_("sb_id128b", [128, 128], BF16)
    ones_bf = sb_("ones_bf", [128, 512], BF16)
    onescol_bf = sb_("onescol_bf", [128, 1], BF16)
    sb1row = sb_("sb1row", [1, H], BF16)
    sb2row = sb_("sb2row", [1, D], BF16)
    rbrow = sb_("rbrow", [1, E], BF16)
    ebst = [sb_("ebst0", [1, 2 * H], BF16), sb_("ebst1", [1, 2 * H], BF16)]
    negw1 = sb_("negw1", [1, 2 * CH], BF16)
    xdbl = sb_("xdbl", [80, TBS], BF16)
    xdbl_c = sb_("xdbl_c", [16, TBS], BF16)
    xdbl_b = sb_("xdbl_b", [16, TBS], BF16)
    carry = [sb_("carry0", [128, 64], F32), sb_("carry1", [128, 64], F32)]
    stat = sb_("stat", [128, 2048], F32)
    rowbf = sb_("rowbf", [128, 4096], BF16)
    wrow = sb_("wrow", [1, 1040], BF16)
    cwrow = sb_("cwrow", [1, 2048], BF16)
    cwT_sb = sb_("cwT_sb", [15, 2048], BF16)
    cpool = sb_("cpool", [1, 4], F32)
    nc.const_aps.register(F32, EPS, cpool[0:1, 0:1])

    AB = Arena(nc, es, "AB", BF16, 69696)
    AFR = Arena(nc, es, "AFR", F32, 4320)
    psA_t = es.enter_context(nc.psum_tensor("psA", [128, 2048], F32))
    psB_t = es.enter_context(nc.psum_tensor("psB", [128, 2048], F32))

    if True:
        op, dma, coll = s.op, s.dma, s.collective

        def psum(arena, lo, hi, rows=128, base=0):
            t = psA_t if arena == "A" else psB_t
            return t[base:base + rows, lo:hi]

        def pskeys(arena, lo, hi):
            return [f"ps{arena}{i}" for i in range(lo // 512, (hi + 511) // 512)]

        dma("pool", "prep2", xpwT.ap().rearrange("p (g e) -> p g e", g=4),
            io["xpwT"].rearrange("(g p) e -> p g e", p=128), writes=["xpwT"])
        dma("pool", "prep2", dtwT.ap().rearrange("r (g c) -> r g c", g=4),
            io["dtwT"].rearrange("r (g c) -> r g c", g=4), writes=["dtwT"])
        dma("pool", "prep2", rw_sb.ap().rearrange("p (g e) -> p g e", g=8),
            io["rw"].rearrange("(g p) e -> p g e", p=128), writes=["rw_sb"])
        dma("sp", "prep", sel16[0:16, :], io["sel16"][:, :], writes=["sel16"])
        dma("sp", "prep", id128[:, :], io["id128"][:, :], writes=["id128"])
        op("act", lambda q: q.activation(id128b[:, :], id128[:, :], AF.Copy),
           reads=["id128"], writes=["id128b"])
        op("dve", lambda q: q.memset(ones_bf[:, :], 1.0), writes=["ones_bf"])
        for g in range(4):
            dma("sp", "prep", convw[:, g * 4:(g + 1) * 4],
                io["convw"][g * 128:(g + 1) * 128, :], writes=["convw"])
            dma("sp", "prep", convb[:, g:g + 1],
                io["convb"][g * 128:(g + 1) * 128].unsqueeze(1), writes=["convb"])
            dma("sp", "prep", dtb[:, g:g + 1],
                io["dtb"][g * 128:(g + 1) * 128].unsqueeze(1), writes=["dtb"])
            dma("sp", "prep", dvec[:, g:g + 1],
                io["dvec"][g * 128:(g + 1) * 128].unsqueeze(1), writes=["dvec"])
        dma("pool", "prep2", sb1row[0:1, :], io["sb1r"][:, :], writes=["sb1row"])
        dma("pool", "prep2", sb2row[0:1, :], io["sb2r"][:, :], writes=["sb2row"])
        dma("pool", "prep2", rbrow[0:1, :], io["rb"][:, :], writes=["rbrow"])
        op("dve", lambda q: q.memset(onescol_bf[:, :], 1.0), writes=["onescol_bf"])
        for k in range(32):
            li_k, isw1 = k // 2, (k % 2 == 0)
            if li_k == 0:
                srcw = io["sw1"] if isw1 else io["sw2"]
            else:
                srcw = (io["ew1"] if isw1 else io["ew2"])[li_k - 1, :, :]
            dma("pool", "wcast", io["wbf"][k, :, :], srcw, writes=["wbf"])
        dma("pool", "wcast", io["ebbf"][:, 0:E * H], io["eb1r"][:, :], writes=["ebbf"])
        dma("pool", "wcast", io["ebbf"][:, E * H:2 * E * H], io["eb2r"][:, :], writes=["ebbf"])
        op("dve", lambda q: q.memset(cpool[0:1, 0:1], EPS), writes=["cpool"])
        op("dve", lambda q: q.memset(carry[0][:, :], 0.0), writes=["carry0"])
        op("dve", lambda q: q.memset(carry[1][:, :], 0.0), writes=["carry1"])
        op("dve", lambda q: q.memset(a_t[0][:, 0:1], 0.0), writes=["a_t0"])
        op("dve", lambda q: q.memset(a_t[1][:, 0:1], 0.0), writes=["a_t1"])
        op("dve", lambda q: q.tensor_single_scalar(ndtb[:, :], dtb[:, :], -1.0, ALU.mult),
           reads=["dtb"], writes=["ndtb"])

        AB.push()
        AFR.push()
        winT = AB.alloc(8 * 1024)
        woutT = AB.alloc(4 * 1024)
        hbf = AB.alloc(8 * TBS)
        Brep = AB.alloc(TBS)
        Crep = AB.alloc(2 * TBS).bitcast(F32)
        b_t = AB.alloc(TBS + 2)
        s_t = [AB.alloc(2 * TBS + 4).bitcast(F32) for _ in range(2)]
        w_t = AB.alloc(TBS)
        sqM = AB.alloc(TBS)
        tmpM = Brep
        ostage = sqM
        x_raw = [AB.alloc(TBS + 4) for _ in range(4)]
        x_s = AB.alloc(4 * TBS)
        sz = AB.alloc(4 * TBS)
        delta = AB.alloc(4 * TBS)
        dx = AB.alloc(4 * TBS)
        yg = AB.alloc(4 * TBS)
        w_t2 = AB.alloc(TBS)
        a_t = [AB.alloc(2 * TBS + 4).bitcast(F32) for _ in range(2)]
        yacc = AB.alloc(4 * TBS)
        bcast_bf = w_t
        stage = AFR.alloc(TBS)
        stage2 = AFR.alloc(TBS)
        ytmp = AFR.alloc(TBS)

        dma("pool", "winT", winT.rearrange("p (g e) -> p g e", g=8),
            io["w_inT"].rearrange("(g p) e -> p g e", p=128), writes=["winT"])
        dma("pool", "woutT", woutT.rearrange("p (g m) -> p g m", g=4),
            io["w_outT"].rearrange("(g p) m -> p g m", p=128), writes=["woutT"])
        dma("sp", "alog", stage[:, 0:64].rearrange("p (g n) -> p g n", g=4),
            io["alog"].rearrange("(g p) n -> p g n", p=128), writes=["stage"])
        op("act", lambda q: q.activation(A_sb[:, :], stage[:, 0:64], AF.Exp),
           reads=["stage"], writes=["A_sb"])
        op("dve", lambda q: q.tensor_single_scalar(A_sb[:, :], A_sb[:, :], -1.0, ALU.mult),
           reads=["A_sb"], writes=["A_sb"])

        for t5 in range(2):
            lo, hi = t5 * 512, (t5 + 1) * 512
            for g in range(8):
                op("pe", lambda q, g=g, lo=lo, hi=hi: q.matmul(
                    psum("A", lo, hi, rows=1), onescol_bf[:, 0:1],
                    winT[:, g * 1024 + lo:g * 1024 + hi],
                    start=(g == 0), stop=(g == 7)),
                    reads=["winT", "onescol_bf"], writes=pskeys("A", lo, hi))
        op("dve", lambda q: q.tensor_single_scalar(
            negw1[0:1, :], psum("A", 0, 1024, rows=1), -1.0, ALU.mult),
            reads=pskeys("A", 0, 1024), writes=["negw1"])

        mstop = {"inproj": 0, "conv": 1, "xproj": 2, "delta": 3, "scan": 5,
                 "outproj": 6, "mamba": 7, "resid": 8, "router": 9,
                 "experts": 10}.get(upto, 99)
        import os
        ntb_run = NTB if mstop >= 7 else 1
        if os.environ.get("NTB_OVERRIDE"):
            ntb_run = int(os.environ["NTB_OVERRIDE"])

        gidx = 0
        for tb in range(ntb_run):
            t0 = tb * TBS
            dma("pool", "hbf", hbf.rearrange("p (g t) -> p g t", g=8),
                io["hT"][:, t0:t0 + TBS].rearrange("(g p) t -> p g t", p=128),
                writes=["hbf"])
            for dg in range(8):
                op("act", lambda q, dg=dg: q.activation(
                    sqM[:, 0:TBS], hbf[:, dg * TBS:(dg + 1) * TBS], AF.Square),
                    reads=["hbf"], writes=["sqM", "ostage"])
                for t5 in range(2):
                    lo, hi = t5 * 512, (t5 + 1) * 512
                    op("pe", lambda q, dg=dg, lo=lo, hi=hi: q.matmul(
                        psum("A", lo, hi, rows=1), onescol_bf[:, 0:1],
                        hbf[:, dg * TBS + lo:dg * TBS + hi],
                        start=(dg == 0), stop=(dg == 7)),
                        reads=["hbf", "onescol_bf"], writes=pskeys("A", lo, hi))
                    op("pe", lambda q, dg=dg, lo=lo, hi=hi: q.matmul(
                        psum("B", lo, hi, rows=1), onescol_bf[:, 0:1],
                        sqM[:, lo:hi], start=(dg == 0), stop=(dg == 7)),
                        reads=["sqM", "onescol_bf"], writes=pskeys("B", lo, hi))
            op("dve", lambda q: q.tensor_single_scalar(
                stat[0:1, 0:TBS], psum("A", 0, TBS, rows=1), 1.0 / D, ALU.mult),
                reads=pskeys("A", 0, TBS), writes=["stat_m1"])
            op("dve", lambda q: q.tensor_single_scalar(
                stage[0:1, 0:TBS], psum("B", 0, TBS, rows=1), 1.0 / D, ALU.mult),
                reads=pskeys("B", 0, TBS), writes=["stage"])
            op("dve", lambda q: q.tensor_tensor(
                stage2[0:1, 0:TBS], stat[0:1, 0:TBS], stat[0:1, 0:TBS], ALU.mult),
                reads=["stat_m1"], writes=["stage2"])
            op("dve", lambda q: q.tensor_tensor(
                stage2[0:1, 0:TBS], stage[0:1, 0:TBS], stage2[0:1, 0:TBS], ALU.subtract),
                reads=["stage", "stage2"], writes=["stage2"])
            op("act", lambda q: q.activation(
                stage2[0:1, 0:TBS], stage2[0:1, 0:TBS], AF.Ln, bias=EPS),
                reads=["stage2", "cpool"], writes=["stage2"])
            op("act", lambda q: q.activation(
                stat[32:33, 0:TBS], stage2[0:1, 0:TBS], AF.Exp, scale=-0.5),
                reads=["stage2"], writes=["stat_r1"])
            op("dve", lambda q: q.tensor_copy(rowbf[0:1, 0:TBS], stat[0:1, 0:TBS]),
               reads=["stat_m1"], writes=["rowbf_m1"])
            op("act", lambda q: q.activation(
                rowbf[32:33, 0:TBS], stat[32:33, 0:TBS], AF.Copy),
                reads=["stat_r1"], writes=["rowbf_r1"])
            for t5 in range(2):
                lo, hi = t5 * 512, (t5 + 1) * 512
                op("pe", lambda q, lo=lo, hi=hi: q.matmul(
                    psum("B", lo, hi), ones_bf[32:33, 0:128],
                    rowbf[32:33, lo:hi], start=True, stop=True),
                    reads=["rowbf_r1", "ones_bf"], writes=pskeys("B", lo, hi))
            op("dve", lambda q: q.tensor_copy(bcast_bf[:, 0:TBS], psum("B", 0, TBS)),
               reads=pskeys("B", 0, TBS), writes=["rstd1b", "w_t"])

            for ec in range(8):
                pa = "A" if ec % 2 == 0 else "B"
                for t5 in range(2):
                    lo, hi = t5 * 512, (t5 + 1) * 512
                    for dg in range(8):
                        op("pe", lambda q, ec=ec, dg=dg, lo=lo, hi=hi, pa=pa: q.matmul(
                            psum(pa, lo, hi),
                            winT[:, dg * 1024 + ec * 128:dg * 1024 + (ec + 1) * 128],
                            hbf[:, dg * TBS + lo:dg * TBS + hi],
                            start=(dg == 0), stop=False),
                            reads=["winT", "hbf"], writes=pskeys(pa, lo, hi))
                    op("pe", lambda q, ec=ec, lo=lo, hi=hi, pa=pa: q.matmul(
                        psum(pa, lo, hi),
                        negw1[0:1, ec * 128:(ec + 1) * 128],
                        rowbf[0:1, lo:hi],
                        start=False, stop=True),
                        reads=["negw1", "rowbf_m1"], writes=pskeys(pa, lo, hi))
                if ec < 4:
                    chg = ec
                    if tb == 0:
                        op("dve", lambda q, chg=chg: q.memset(x_raw[chg][:, 0:3], 0.0),
                           writes=[f"x_raw{chg}"])
                    else:
                        op("dve", lambda q, chg=chg: q.tensor_copy(
                            x_raw[chg][:, 0:3], x_raw[chg][:, TBS:TBS + 3]),
                            reads=[f"x_raw{chg}"], writes=[f"x_raw{chg}"])
                    op("dve", lambda q, chg=chg, pa=pa: q.tensor_tensor(
                        x_raw[chg][:, 3:3 + TBS], psum(pa, 0, TBS),
                        bcast_bf[:, 0:TBS], ALU.mult),
                        reads=pskeys(pa, 0, TBS) + ["rstd1b"], writes=[f"x_raw{chg}"])
                else:
                    chg = ec - 4
                    op("dve", lambda q, pa=pa: q.tensor_tensor(
                        stage[:, 0:TBS], psum(pa, 0, TBS), bcast_bf[:, 0:TBS], ALU.mult),
                        reads=pskeys(pa, 0, TBS) + ["rstd1b"], writes=["stage"])
                    op("act", lambda q, chg=chg: q.activation(
                        sz[:, chg * TBS:(chg + 1) * TBS], stage[:, 0:TBS], AF.Silu),
                        reads=["stage"], writes=[f"sz{chg}"])
            if mstop <= 0:
                break

            for chg in range(4):
                op("dve", lambda q, chg=chg: q.tensor_single_scalar(
                    stage2[:, 0:TBS], x_raw[chg][:, 0:TBS],
                    convw[:, chg * 4:chg * 4 + 1], ALU.mult),
                    reads=[f"x_raw{chg}"], writes=["stage2"], scalar_reads=["convw"])
                for k in range(1, 4):
                    op("dve", lambda q, chg=chg, k=k: q.scalar_tensor_tensor(
                        stage2[:, 0:TBS], x_raw[chg][:, k:k + TBS],
                        convw[:, chg * 4 + k:chg * 4 + k + 1], stage2[:, 0:TBS],
                        ALU.mult, ALU.add),
                        reads=[f"x_raw{chg}", "stage2"], writes=["stage2"],
                        scalar_reads=["convw"])
                op("act", lambda q, chg=chg: q.activation(
                    x_s[:, chg * TBS:(chg + 1) * TBS], stage2[:, 0:TBS], AF.Silu,
                    bias=convb[:, chg:chg + 1]),
                    reads=["stage2", "convb"], writes=[f"x_s{chg}"])
            if mstop <= 1:
                break

            for t5 in range(2):
                lo, hi = t5 * 512, (t5 + 1) * 512
                for chg in range(4):
                    op("pe", lambda q, chg=chg, lo=lo, hi=hi: q.matmul(
                        psum("A", lo, hi, rows=80),
                        xpwT[:, chg * 96:chg * 96 + 80],
                        x_s[:, chg * TBS + lo:chg * TBS + hi],
                        start=(chg == 0), stop=(chg == 3)),
                        reads=["xpwT", f"x_s{chg}"], writes=pskeys("A", lo, hi))
                    op("pe", lambda q, chg=chg, lo=lo, hi=hi: q.matmul(
                        psum("B", lo, hi, rows=16),
                        xpwT[:, chg * 96 + 80:chg * 96 + 96],
                        x_s[:, chg * TBS + lo:chg * TBS + hi],
                        start=(chg == 0), stop=(chg == 3)),
                        reads=["xpwT", f"x_s{chg}"], writes=pskeys("B", lo, hi))
            op("dve", lambda q: q.tensor_copy(xdbl[0:80, :], psum("A", 0, TBS, rows=80)),
               reads=pskeys("A", 0, TBS), writes=["xdbl"])
            op("dve", lambda q: q.tensor_copy(xdbl_c[0:16, :], psum("B", 0, TBS, rows=16)),
               reads=pskeys("B", 0, TBS), writes=["xdbl_c"])
            dma("sp", f"cxi{tb}", io[f"ccx_in{tb}"][0:80, :], xdbl[0:80, :], reads=["xdbl"])
            dma("sp", f"cxi{tb}", io[f"ccx_in{tb}"][80:96, :], xdbl_c[0:16, :], reads=["xdbl_c"])
            s.buf[f"ccxi{tb}"] = {"w": ("dma", f"cxi{tb}", 2), "r": []}
            coll(lambda q, tb=tb: q.collective_compute(
                "AllReduce", ALU.add, replica_groups=GROUPS,
                ins=[io[f"ccx_in{tb}"].opt()], outs=[io[f"ccx_out{tb}"].opt()]),
                f"arx{tb}", reads=[f"ccxi{tb}"], writes=[f"ccxo{tb}"])
            dma("sp", f"cxo{tb}", xdbl[0:64, :], io[f"ccx_out{tb}"][0:64, :],
                reads=[f"ccxo{tb}"], writes=["xdbl"])
            dma("sp", f"cxo{tb}", xdbl_b[0:16, :], io[f"ccx_out{tb}"][64:80, :],
                reads=[f"ccxo{tb}"], writes=["xdbl_b"])
            dma("sp", f"cxo{tb}", xdbl_c[0:16, :], io[f"ccx_out{tb}"][80:96, :],
                reads=[f"ccxo{tb}"], writes=["xdbl_c"])
            if mstop <= 2:
                break

            for chg in range(4):
                pa = "A" if chg % 2 == 0 else "B"
                for t5 in range(2):
                    lo, hi = t5 * 512, (t5 + 1) * 512
                    op("pe", lambda q, chg=chg, lo=lo, hi=hi, pa=pa: q.matmul(
                        psum(pa, lo, hi), dtwT[0:64, chg * 128:(chg + 1) * 128],
                        xdbl[0:64, lo:hi], start=True, stop=True),
                        reads=["dtwT", "xdbl"], writes=pskeys(pa, lo, hi))
                op("act", lambda q, chg=chg, pa=pa: q.activation(
                    stage[:, 0:TBS], psum(pa, 0, TBS), AF.Relu, bias=dtb[:, chg:chg + 1]),
                    reads=pskeys(pa, 0, TBS) + ["dtb"], writes=["stage"])
                op("act", lambda q, chg=chg, pa=pa: q.activation(
                    stage2[:, 0:TBS], psum(pa, 0, TBS), AF.Relu,
                    bias=ndtb[:, chg:chg + 1], scale=-1.0),
                    reads=pskeys(pa, 0, TBS) + ["ndtb"], writes=["stage2"])
                op("dve", lambda q: q.tensor_tensor(
                    sqM[:, 0:TBS], stage[:, 0:TBS], stage2[:, 0:TBS], ALU.add),
                    reads=["stage", "stage2"], writes=["sqM", "ostage"])
                op("act", lambda q: q.activation(tmpM[:, 0:TBS], sqM[:, 0:TBS], AF.Exp, scale=-1.0),
                   reads=["sqM"], writes=["tmpM", "Brep"])
                op("act", lambda q: q.activation(sqM[:, 0:TBS], tmpM[:, 0:TBS], AF.Ln, bias=1.0),
                   reads=["tmpM"], writes=["sqM", "ostage"])
                op("dve", lambda q, chg=chg: q.tensor_tensor(
                    delta[:, chg * TBS:(chg + 1) * TBS], stage[:, 0:TBS], sqM[:, 0:TBS], ALU.add),
                    reads=["stage", "sqM"], writes=[f"delta{chg}"])
                op("dve", lambda q, chg=chg: q.tensor_tensor(
                    dx[:, chg * TBS:(chg + 1) * TBS],
                    delta[:, chg * TBS:(chg + 1) * TBS],
                    x_s[:, chg * TBS:(chg + 1) * TBS], ALU.mult),
                    reads=[f"delta{chg}", f"x_s{chg}"], writes=[f"dx{chg}"])
            if mstop <= 3:
                break

            cprev, ccur = carry[gidx % 2], carry[(gidx + 1) % 2]
            kprev, kcur = f"carry{gidx % 2}", f"carry{(gidx + 1) % 2}"
            for chg in range(4):
                op("dve", lambda q, chg=chg: q.tensor_single_scalar(
                    yacc[:, chg * TBS:(chg + 1) * TBS], x_s[:, chg * TBS:(chg + 1) * TBS],
                    dvec[:, chg:chg + 1], ALU.mult),
                    reads=[f"x_s{chg}"], writes=[f"yacc{chg}"], scalar_reads=["dvec"])

            def bc_mms(n):
                for t5 in range(2):
                    lo, hi = t5 * 512, (t5 + 1) * 512
                    op("pe", lambda q, n=n, lo=lo, hi=hi: q.matmul(
                        psum("A", lo, hi), sel16[0:16, n * 128:(n + 1) * 128],
                        xdbl_b[0:16, lo:hi], start=True, stop=True),
                        reads=["sel16", "xdbl_b", "Brep"], writes=pskeys("A", lo, hi))
                    op("pe", lambda q, n=n, lo=lo, hi=hi: q.matmul(
                        psum("B", lo, hi), sel16[0:16, n * 128:(n + 1) * 128],
                        xdbl_c[0:16, lo:hi], start=True, stop=True),
                        reads=["sel16", "xdbl_c", "Crep"], writes=pskeys("B", lo, hi))

            for pr in range(2):
                for chg in (2 * pr, 2 * pr + 1):
                    pa = "A" if chg % 2 == 0 else "B"
                    for t5 in range(2):
                        lo, hi = t5 * 512, (t5 + 1) * 512
                        op("pe", lambda q, chg=chg, pa=pa, lo=lo, hi=hi: q.matmul(
                            psum(pa, 1024 + lo, 1024 + hi), id128b[:, :],
                            yacc[:, chg * TBS + lo:chg * TBS + hi],
                            start=True, stop=False),
                            reads=[f"yacc{chg}", "id128b"],
                            writes=pskeys(pa, 1024 + lo, 1024 + hi))
                bc_mms(0)
                for n in range(16):
                    op("act", lambda q: q.activation(Brep[:, :], psum("A", 0, TBS), AF.Copy),
                       reads=pskeys("A", 0, TBS), writes=["Brep"])
                    op("act", lambda q, chg=2 * pr, n=n: q.activation(
                        a_t[0][:, 1:TBS + 1], delta[:, chg * TBS:(chg + 1) * TBS], AF.Exp,
                        scale=A_sb[:, chg * 16 + n:chg * 16 + n + 1]),
                        reads=[f"delta{2 * pr}"], writes=["a_t0"], scalar_reads=["A_sb"])
                    op("act", lambda q: q.activation(Crep[:, :], psum("B", 0, TBS), AF.Copy),
                       reads=pskeys("B", 0, TBS), writes=["Crep"])
                    op("act", lambda q, chg=2 * pr + 1, n=n: q.activation(
                        a_t[1][:, 1:TBS + 1], delta[:, chg * TBS:(chg + 1) * TBS], AF.Exp,
                        scale=A_sb[:, chg * 16 + n:chg * 16 + n + 1]),
                        reads=[f"delta{2 * pr + 1}"], writes=["a_t1"], scalar_reads=["A_sb"])
                    if n < 15:
                        bc_mms(n + 1)
                    for chg in (2 * pr, 2 * pr + 1):
                        col = chg * 16 + n
                        cp_ = chg % 2
                        st, stk = s_t[cp_], f"s_t{cp_}"
                        op("dve", lambda q, col=col, cprev=cprev: q.tensor_copy(
                            b_t[:, 0:1], cprev[:, col:col + 1]),
                            reads=[kprev], writes=["b_t"])
                        op("dve", lambda q, chg=chg: q.tensor_tensor(
                            b_t[:, 1:TBS + 1], dx[:, chg * TBS:(chg + 1) * TBS],
                            Brep[:, :], ALU.mult),
                            reads=[f"dx{chg}", "Brep"], writes=["b_t"])
                        init, initreads = cprev[:, col:col + 1], [kprev]
                        op("dve", lambda q, init=init, st=st, cp_=cp_: q.tensor_tensor_scan(
                            st[:, 0:TBS + 1], a_t[cp_][:, 0:TBS + 1], b_t[:, 0:TBS + 1],
                            init, ALU.mult, ALU.add),
                            reads=[f"a_t{cp_}", "b_t"], writes=[stk], scalar_reads=initreads)
                        wdst, wkey = (w_t, "w_t") if cp_ == 0 else (w_t2, "w_t2")
                        wk2 = [wkey, "rstd1b"] if wkey == "w_t" else [wkey]
                        op("pool", lambda q, wdst=wdst, st=st: q.tensor_tensor(
                            wdst[:, :], st[:, 1:TBS + 1], Crep[:, :], ALU.mult),
                            reads=[stk, "Crep"], writes=wk2)
                        op("pool", lambda q, col=col, ccur=ccur, st=st: q.tensor_copy(
                            ccur[:, col:col + 1], st[:, TBS:TBS + 1]),
                            reads=[stk], writes=[kcur])
                        pa = "A" if cp_ == 0 else "B"
                        for t5 in range(2):
                            lo, hi = t5 * 512, (t5 + 1) * 512
                            op("pe", lambda q, pa=pa, lo=lo, hi=hi, wdst=wdst, n=n: q.matmul(
                                psum(pa, 1024 + lo, 1024 + hi), id128b[:, :],
                                wdst[:, lo:hi], start=False, stop=(n == 15)),
                                reads=[wkey, "id128b"],
                                writes=pskeys(pa, 1024 + lo, 1024 + hi))
                for chg in (2 * pr, 2 * pr + 1):
                    pa = "A" if chg % 2 == 0 else "B"
                    op("dve", lambda q, chg=chg, pa=pa: q.tensor_tensor(
                        yg[:, chg * TBS:(chg + 1) * TBS], psum(pa, 1024, 2048),
                        sz[:, chg * TBS:(chg + 1) * TBS], ALU.mult),
                        reads=pskeys(pa, 1024, 2048) + [f"sz{chg}"], writes=[f"yg{chg}"])
            gidx += 1
            if mstop <= 5:
                break

            for mg in range(8):
                pa = "A" if mg % 2 == 0 else "B"
                for t5 in range(2):
                    lo, hi = t5 * 512, (t5 + 1) * 512
                    for chg in range(4):
                        op("pe", lambda q, mg=mg, chg=chg, lo=lo, hi=hi, pa=pa: q.matmul(
                            psum(pa, lo, hi),
                            woutT[:, chg * 1024 + mg * 128:chg * 1024 + (mg + 1) * 128],
                            yg[:, chg * TBS + lo:chg * TBS + hi],
                            start=(chg == 0), stop=(chg == 3)),
                            reads=["woutT", f"yg{chg}"], writes=pskeys(pa, lo, hi))
                op("dve", lambda q, mg=mg, pa=pa: q.tensor_copy(
                    ostage[:, 0:TBS], psum(pa, 0, TBS)),
                    reads=pskeys(pa, 0, TBS), writes=["ostage", "sqM"])
                dma("sp", "ccm", io["ccm_in"][tb * D + mg * 128:tb * D + (mg + 1) * 128, :],
                    ostage[:, 0:TBS], reads=["ostage"])

        if mstop >= 8:
            s.buf["ccmi"] = {"w": ("dma", "ccm", s.dma_cnt.get("ccm", 0)), "r": []}
            coll(lambda q: q.collective_compute(
                "ReduceScatter", ALU.add, replica_groups=GROUPS,
                ins=[io["ccm_in"].opt()], outs=[io["ccm_out"].opt()]),
                "rsm", reads=["ccmi"], writes=["ccmo"])
        s.barrier()
        AB.pop()
        AFR.pop()

        W2K = 2048

        def stats2048(src, srckey_fn, sqt, sqkey, tag, scrA, scrAk, scrB, scrBk):
            for dg in range(8):
                op("act", lambda q, dg=dg: q.activation(
                    sqt[:, 0:W2K], src[:, dg * W2K:(dg + 1) * W2K], AF.Square),
                    reads=srckey_fn(dg), writes=[sqkey])
                for t5 in range(4):
                    lo, hi = t5 * 512, (t5 + 1) * 512
                    op("pe", lambda q, dg=dg, lo=lo, hi=hi: q.matmul(
                        psum("A", lo, hi, rows=1), onescol_bf[:, 0:1],
                        src[:, dg * W2K + lo:dg * W2K + hi],
                        start=(dg == 0), stop=(dg == 7)),
                        reads=srckey_fn(dg) + ["onescol_bf"], writes=pskeys("A", lo, hi))
                    op("pe", lambda q, dg=dg, lo=lo, hi=hi: q.matmul(
                        psum("B", lo, hi, rows=1), onescol_bf[:, 0:1],
                        sqt[:, lo:hi], start=(dg == 0), stop=(dg == 7)),
                        reads=[sqkey, "onescol_bf"], writes=pskeys("B", lo, hi))
            op("dve", lambda q: q.tensor_single_scalar(
                stat[0:1, 0:W2K], psum("A", 0, W2K, rows=1), 1.0 / D, ALU.mult),
                reads=pskeys("A", 0, W2K), writes=["stat_m" + tag])
            op("dve", lambda q: q.tensor_single_scalar(
                scrA[0:1, 0:W2K], psum("B", 0, W2K, rows=1), 1.0 / D, ALU.mult),
                reads=pskeys("B", 0, W2K), writes=[scrAk])
            op("dve", lambda q: q.tensor_tensor(
                scrB[0:1, 0:W2K], stat[0:1, 0:W2K], stat[0:1, 0:W2K], ALU.mult),
                reads=["stat_m" + tag], writes=[scrBk])
            op("dve", lambda q: q.tensor_tensor(
                scrB[0:1, 0:W2K], scrA[0:1, 0:W2K], scrB[0:1, 0:W2K], ALU.subtract),
                reads=[scrAk, scrBk], writes=[scrBk])
            op("act", lambda q: q.activation(
                scrB[0:1, 0:W2K], scrB[0:1, 0:W2K], AF.Ln, bias=EPS),
                reads=[scrBk, "cpool"], writes=[scrBk])
            op("act", lambda q: q.activation(
                stat[32:33, 0:W2K], scrB[0:1, 0:W2K], AF.Exp, scale=-0.5),
                reads=[scrBk], writes=["stat_r" + tag])
            op("dve", lambda q: q.tensor_copy(rowbf[0:1, 0:W2K], stat[0:1, 0:W2K]),
               reads=["stat_m" + tag], writes=["rowbf_m" + tag])
            op("act", lambda q: q.activation(
                rowbf[32:33, 0:W2K], stat[32:33, 0:W2K], AF.Copy),
                reads=["stat_r" + tag], writes=["rowbf_r" + tag])

        def brow(dst, dstkey, rowpart, rowkey):
            for t5 in range(4):
                lo, hi = t5 * 512, (t5 + 1) * 512
                op("pe", lambda q, lo=lo, hi=hi: q.matmul(
                    psum("B", lo, hi), ones_bf[rowpart:rowpart + 1, 0:128],
                    rowbf[rowpart:rowpart + 1, lo:hi], start=True, stop=True),
                    reads=[rowkey, "ones_bf"], writes=pskeys("B", lo, hi))
            op("dve", lambda q: q.tensor_copy(dst[:, 0:W2K], psum("B", 0, W2K)),
               reads=pskeys("B", 0, W2K), writes=[dstkey])

        if mstop >= 8:
            resid = AB.alloc(16 * 1024)
            AB.push()
            AFR.push()
            mo = AB.alloc(16 * 1024)
            bc2 = AB.alloc(2 * W2K)
            sqR = AB.alloc(W2K)
            hst = [AFR.alloc(W2K), AFR.alloc(W2K)]
            for tbl in range(2):
                dma("sp", "moin",
                    mo.rearrange("p (g tb t) -> p g tb t", g=8, tb=2)[:, :, tbl, :],
                    io["ccm_out"][tbl * 1024:(tbl + 1) * 1024, :]
                        .rearrange("(g p) t -> p g t", p=128),
                    reads=["ccmo"], writes=["mo"])
            stats2048(mo, lambda dg: ["mo"], sqR, "sqR", "2", hst[0], "hrs0", hst[1], "hrs1")
            brow(bc2[:, 0:W2K], "m2b", 0, "rowbf_m2")
            brow(bc2[:, W2K:2 * W2K], "r2b", 32, "rowbf_r2")
            for dg in range(8):
                sl = dg % 2
                dma("sp", f"hr{sl}", hst[sl][:, 0:W2K],
                    io["hres"][dg * 128:(dg + 1) * 128, :], writes=[f"hrs{sl}"])
                op("dve", lambda q, dg=dg: q.tensor_tensor(
                    sqR[:, 0:W2K], mo[:, dg * W2K:(dg + 1) * W2K], bc2[:, 0:W2K], ALU.subtract),
                    reads=["mo", "m2b"], writes=["sqR"])
                op("dve", lambda q: q.tensor_tensor(
                    sqR[:, 0:W2K], sqR[:, 0:W2K], bc2[:, W2K:2 * W2K], ALU.mult),
                    reads=["sqR", "r2b"], writes=["sqR"])
                op("dve", lambda q, dg=dg, sl=sl: q.tensor_tensor(
                    resid[:, dg * W2K:(dg + 1) * W2K], hst[sl][:, 0:W2K], sqR[:, 0:W2K], ALU.add),
                    reads=[f"hrs{sl}", "sqR"], writes=[f"resid{dg}"])
            s.barrier()
            AB.pop()
            AFR.pop()

        if mstop >= 9:
            AB.push()
            AFR.push()
            wslot = [AB.alloc(8 * 1024) for _ in range(3)]
            h_sc = AB.alloc(8 * 1024)
            acc = AB.alloc(8 * W2K)
            cwsc = AB.alloc(W2K)
            sqF = cwsc
            bcF = AB.alloc(W2K)
            m4b = wslot[0][:, 0:W2K]
            r4b = wslot[0][:, W2K:2 * W2K]
            lg_em = AFR.alloc(W2K)
            smx = AFR.alloc(64)
            finstage = AFR.alloc(W2K)
            outst = AFR.alloc(128)

            def load_w(k):
                sl = k % 3
                dma("sp", f"ws{sl}", wslot[sl].rearrange("p (g h) -> p g h", g=8),
                    io["wbf"][k, :, :].rearrange("(g p) h -> p g h", p=128),
                    reads=["wbf"], writes=[f"wslot{sl}"])

            stats2048(resid, lambda dg: [f"resid{dg}"], sqF, "cwsc", "3", lg_em, "lg_em", finstage, "finstage")
            op("act", lambda q: q.activation(
                rowbf[0:1, 2048:2048 + W2K], finstage[0:1, 0:W2K], AF.Exp, scale=0.5),
                reads=["finstage"], writes=["rowbf_s3"])
            brow(bcF[:, 0:W2K], "r3b", 32, "rowbf_r3")

            for g in range(8):
                op("pe", lambda q, g=g: q.matmul(
                    psum("A", 0, E, rows=1), onescol_bf[:, 0:1],
                    rw_sb[:, g * E:(g + 1) * E],
                    start=(g == 0), stop=(g == 7)),
                    reads=["rw_sb", "onescol_bf"], writes=pskeys("A", 0, 512))
            op("dve", lambda q: q.tensor_single_scalar(
                wrow[0:1, 1024:1024 + E], psum("A", 0, E, rows=1), -1.0, ALU.mult),
                reads=pskeys("A", 0, 512), writes=["w1r"])
            for tc in range(4):
                lo, hi = tc * 512, (tc + 1) * 512
                for g in range(8):
                    op("pe", lambda q, g=g, tc=tc: q.matmul(
                        psum("A", tc * 512, (tc + 1) * 512, rows=15),
                        rw_sb[:, g * E:(g + 1) * E],
                        resid[:, g * W2K + tc * 512:g * W2K + (tc + 1) * 512],
                        start=(g == 0), stop=False),
                        reads=["rw_sb", f"resid{g}"], writes=pskeys("A", tc * 512, (tc + 1) * 512))
                op("pe", lambda q, tc=tc: q.matmul(
                    psum("A", tc * 512, (tc + 1) * 512, rows=15),
                    wrow[0:1, 1024:1024 + E],
                    rowbf[0:1, tc * 512:(tc + 1) * 512],
                    start=False, stop=False),
                    reads=["w1r", "rowbf_m3"], writes=pskeys("A", tc * 512, (tc + 1) * 512))
                op("pe", lambda q, tc=tc: q.matmul(
                    psum("A", tc * 512, (tc + 1) * 512, rows=15),
                    rbrow[0:1, 0:E],
                    rowbf[0:1, 2048 + tc * 512:2048 + (tc + 1) * 512],
                    start=False, stop=True),
                    reads=["rbrow", "rowbf_s3"], writes=pskeys("A", tc * 512, (tc + 1) * 512))
            op("dve", lambda q: q.tensor_tensor(
                lg_em[0:15, 0:W2K], psum("A", 0, W2K, rows=15), bcF[0:15, 0:W2K], ALU.mult),
                reads=pskeys("A", 0, W2K) + ["r3b"], writes=["lg_em"])

            for tt in range(16):
                op("pe", lambda q, tt=tt: q.transpose(
                    psum("B", 0, 15),
                    lg_em[0:15, tt * 128:(tt + 1) * 128], id128[0:15, 0:15]),
                    reads=["lg_em", "id128"], writes=pskeys("B", 0, 128))
                op("dve", lambda q: q.tensor_copy(smx[:, 0:15], psum("B", 0, 15)),
                   reads=pskeys("B", 0, 128), writes=["lgt"])
                op("dve", lambda q: q.reduce_max(smx[:, 15:16], smx[:, 0:15], AX.X, negate=True),
                   reads=["lgt"], writes=["negm"])
                op("act", lambda q: q.activation(smx[:, 16:31], smx[:, 0:15], AF.Exp,
                                                 bias=smx[:, 15:16]),
                   reads=["lgt"], writes=["ex"], scalar_reads=["negm"])
                op("dve", lambda q: q.reduce_sum(smx[:, 31:32], smx[:, 16:31], AX.X),
                   reads=["ex"], writes=["ssum"])
                op("dve", lambda q: q.reciprocal(smx[:, 32:33], smx[:, 31:32]),
                   reads=["ssum"], writes=["rec"])
                op("dve", lambda q: q.tensor_single_scalar(
                    smx[:, 0:15], smx[:, 16:31], smx[:, 32:33], ALU.mult),
                    reads=["ex"], writes=["probs"], scalar_reads=["rec"])
                op("dve", lambda q: q.reduce_max(smx[:, 15:16], smx[:, 0:15], AX.X),
                   reads=["probs"], writes=["m1"])
                op("dve", lambda q: q.tensor_single_scalar(
                    smx[:, 16:31], smx[:, 0:15], smx[:, 15:16], ALU.is_ge),
                    reads=["probs"], writes=["g1"], scalar_reads=["m1"])
                op("dve", lambda q: q.tensor_scalar(
                    smx[:, 16:31], smx[:, 16:31], -1.0, 1.0, ALU.mult, ALU.add),
                    reads=["g1"], writes=["g1"])
                op("dve", lambda q: q.tensor_tensor(
                    smx[:, 16:31], smx[:, 0:15], smx[:, 16:31], ALU.mult),
                    reads=["probs", "g1"], writes=["p1"])
                op("dve", lambda q: q.reduce_max(smx[:, 15:16], smx[:, 16:31], AX.X),
                   reads=["p1"], writes=["m2"])
                op("dve", lambda q: q.tensor_single_scalar(
                    smx[:, 33:48], smx[:, 16:31], smx[:, 15:16], ALU.is_ge),
                    reads=["p1"], writes=["g2"], scalar_reads=["m2"])
                op("dve", lambda q: q.tensor_scalar(
                    smx[:, 33:48], smx[:, 33:48], -1.0, 1.0, ALU.mult, ALU.add),
                    reads=["g2"], writes=["g2"])
                op("dve", lambda q: q.tensor_tensor(
                    smx[:, 33:48], smx[:, 16:31], smx[:, 33:48], ALU.mult),
                    reads=["p1", "g2"], writes=["p2"])
                op("dve", lambda q: q.reduce_max(smx[:, 15:16], smx[:, 33:48], AX.X),
                   reads=["p2"], writes=["m3"])
                op("dve", lambda q: q.tensor_single_scalar(
                    smx[:, 33:48], smx[:, 0:15], smx[:, 15:16], ALU.is_ge),
                    reads=["probs"], writes=["gk"], scalar_reads=["m3"])
                op("dve", lambda q: q.tensor_tensor(
                    smx[:, 0:15], smx[:, 0:15], smx[:, 33:48], ALU.mult),
                    reads=["probs", "gk"], writes=["cw"])
                op("pe", lambda q, tt=tt: q.transpose(
                    psum("B", 512, 640, rows=15),
                    smx[:, 0:15], id128[:, :]),
                    reads=["cw", "id128"], writes=pskeys("B", 512, 1024))
                op("act", lambda q, tt=tt: q.activation(
                    cwT_sb[0:15, tt * 128:(tt + 1) * 128], psum("B", 512, 640, rows=15),
                    AF.Copy),
                    reads=pskeys("B", 512, 1024), writes=["cwT"])

            for k in range(2):
                load_w(k)
            import os as _os
            _nli = int(_os.environ.get("NLI", "16"))
            for li in (range(_nli) if mstop > 9 else []):
                s1, s2 = (2 * li) % 3, (2 * li + 1) % 3
                slb = li % 2
                if 2 * li + 2 < 2 * _nli:
                    load_w(2 * li + 2)
                if li > 0:
                    dma("sp", f"eb{slb}", ebst[slb][0:1, 0:H],
                        io["ebbf"][:, (li - 1) * H:li * H],
                        reads=["ebbf"], writes=[f"ebr{slb}"])
                    dma("sp", f"eb{slb}", ebst[slb][0:1, H:2 * H],
                        io["ebbf"][:, E * H + (li - 1) * D:E * H + li * D],
                        reads=["ebbf"], writes=[f"ebr{slb}"])
                for t5 in range(2):
                    lo, hi = t5 * 512, (t5 + 1) * 512
                    for g in range(8):
                        op("pe", lambda q, g=g, lo=lo, hi=hi, s1=s1: q.matmul(
                            psum("B", 1024 + lo, 1024 + hi, rows=1), onescol_bf[:, 0:1],
                            wslot[s1][:, g * 1024 + lo:g * 1024 + hi],
                            start=(g == 0), stop=(g == 7)),
                            reads=[f"wslot{s1}", "onescol_bf"],
                            writes=pskeys("B", 1024 + lo, 1024 + hi))
                op("dve", lambda q: q.tensor_single_scalar(
                    wrow[0:1, 0:1024], psum("B", 1024, 2048, rows=1), -1.0, ALU.mult),
                    reads=pskeys("B", 1024, 2048), writes=["w1e"])
                if li > 0:
                    e = li - 1
                    for tc in range(4):
                        op("pe", lambda q, e=e, tc=tc: q.matmul(
                            psum("A", tc * 512, (tc + 1) * 512),
                            sel16[0:15, e * 128:(e + 1) * 128],
                            cwT_sb[0:15, tc * 512:(tc + 1) * 512],
                            start=True, stop=True),
                            reads=["sel16", "cwT"], writes=pskeys("A", tc * 512, (tc + 1) * 512))
                    op("act", lambda q: q.activation(
                        cwrow[0:1, 0:W2K], psum("A", 0, W2K, rows=1), AF.Copy),
                        reads=pskeys("A", 0, W2K), writes=["cwrow"])
                    op("dve", lambda q: q.tensor_tensor(
                        cwsc[:, 0:W2K], psum("A", 0, W2K), bcF[:, 0:W2K], ALU.mult),
                        reads=pskeys("A", 0, W2K) + ["r3b", "cwrow"], writes=["cwsc"])
                for tch in range(2):
                    toff = tch * 1024
                    for hc in range(8):
                        pa = "A" if hc % 2 == 0 else "B"
                        for tc in range(2):
                            lo, hi = tc * 512, (tc + 1) * 512
                            for g in range(8):
                                op("pe", lambda q, g=g, hc=hc, lo=lo, hi=hi, pa=pa,
                                   s1=s1, toff=toff: q.matmul(
                                    psum(pa, lo, hi),
                                    wslot[s1][:, g * 1024 + hc * 128:g * 1024 + (hc + 1) * 128],
                                    resid[:, g * W2K + toff + lo:g * W2K + toff + hi],
                                    start=(g == 0), stop=False),
                                    reads=[f"wslot{s1}", f"resid{g}"], writes=pskeys(pa, lo, hi))
                            op("pe", lambda q, hc=hc, lo=lo, hi=hi, pa=pa, toff=toff: q.matmul(
                                psum(pa, lo, hi), wrow[0:1, hc * 128:(hc + 1) * 128],
                                rowbf[0:1, toff + lo:toff + hi], start=False, stop=False),
                                reads=["w1e", "rowbf_m3"], writes=pskeys(pa, lo, hi))
                            brow_ap = (sb1row[0:1, hc * 128:(hc + 1) * 128] if li == 0
                                       else ebst[slb][0:1, hc * 128:(hc + 1) * 128])
                            brow_key = "sb1row" if li == 0 else f"ebr{slb}"
                            op("pe", lambda q, brow_ap=brow_ap, lo=lo, hi=hi, pa=pa,
                               toff=toff: q.matmul(
                                psum(pa, lo, hi), brow_ap,
                                rowbf[0:1, 2048 + toff + lo:2048 + toff + hi], start=False, stop=True),
                                reads=[brow_key, "rowbf_s3"], writes=pskeys(pa, lo, hi))
                        op("act", lambda q, hc=hc, pa=pa: q.activation(
                            h_sc[:, hc * 1024:(hc + 1) * 1024], psum(pa, 0, 1024), AF.Relu),
                            reads=pskeys(pa, 0, 1024), writes=[f"hsc{hc}"])
                        sc_key = "cwsc" if li > 0 else "r3b"
                        sc_tile = cwsc if li > 0 else bcF
                        op("dve", lambda q, hc=hc, sc_tile=sc_tile, toff=toff: q.tensor_tensor(
                            h_sc[:, hc * 1024:(hc + 1) * 1024], h_sc[:, hc * 1024:(hc + 1) * 1024],
                            sc_tile[:, toff:toff + 1024], ALU.mult),
                            reads=[f"hsc{hc}", sc_key], writes=[f"hsc{hc}"])
                    for dt in range(8):
                        pa = "A" if dt % 2 == 0 else "B"
                        for tc in range(2):
                            lo, hi = tc * 512, (tc + 1) * 512
                            for hc in range(8):
                                op("pe", lambda q, hc=hc, dt=dt, lo=lo, hi=hi, pa=pa,
                                   s2=s2: q.matmul(
                                    psum(pa, lo, hi),
                                    wslot[s2][:, hc * 1024 + dt * 128:hc * 1024 + (dt + 1) * 128],
                                    h_sc[:, hc * 1024 + lo:hc * 1024 + hi],
                                    start=(hc == 0), stop=False),
                                    reads=[f"wslot{s2}", f"hsc{hc}"], writes=pskeys(pa, lo, hi))
                            if li == 0:
                                op("pe", lambda q, dt=dt, lo=lo, hi=hi, pa=pa: q.matmul(
                                    psum(pa, lo, hi),
                                    sb2row[0:1, dt * 128:(dt + 1) * 128],
                                    ones_bf[0:1, 0:512], start=False, stop=True),
                                    reads=["sb2row", "ones_bf"], writes=pskeys(pa, lo, hi))
                            else:
                                op("pe", lambda q, dt=dt, lo=lo, hi=hi, pa=pa, slb=slb,
                                   toff=toff: q.matmul(
                                    psum(pa, lo, hi),
                                    ebst[slb][0:1, H + dt * 128:H + (dt + 1) * 128],
                                    cwrow[0:1, toff + lo:toff + hi], start=False, stop=True),
                                    reads=[f"ebr{slb}", "cwrow"], writes=pskeys(pa, lo, hi))
                        if li == 0:
                            op("dve", lambda q, dt=dt, pa=pa, toff=toff: q.tensor_copy(
                                acc[:, dt * W2K + toff:dt * W2K + toff + 1024],
                                psum(pa, 0, 1024)),
                                reads=pskeys(pa, 0, 1024), writes=[f"acc{dt}"])
                        else:
                            op("dve", lambda q, dt=dt, pa=pa, toff=toff: q.tensor_tensor(
                                acc[:, dt * W2K + toff:dt * W2K + toff + 1024],
                                acc[:, dt * W2K + toff:dt * W2K + toff + 1024],
                                psum(pa, 0, 1024), ALU.add),
                                reads=[f"acc{dt}"] + pskeys(pa, 0, 1024), writes=[f"acc{dt}"])
                if 2 * li + 3 < 2 * _nli and tch == 1 and dt == 7:
                    load_w(2 * li + 3)


            do_final = mstop > 10
            if do_final:
                stats2048(acc, lambda dg: [f"acc{dg}"], sqF, "cwsc", "4", lg_em, "lg_em", finstage, "finstage")
                brow(m4b, "wslot0", 0, "rowbf_m4")
            
                brow(r4b, "wslot0", 32, "rowbf_r4")
                for dg in range(8):
                    op("dve", lambda q, dg=dg: q.tensor_tensor(
                        sqF[:, 0:W2K], acc[:, dg * W2K:(dg + 1) * W2K],
                        m4b, ALU.subtract),
                        reads=[f"acc{dg}", "wslot0"], writes=["sqFn"])
                    op("dve", lambda q: q.tensor_tensor(
                        sqF[:, 0:W2K], sqF[:, 0:W2K], r4b, ALU.mult),
                        reads=["sqFn", "wslot0"], writes=["sqFn"])
                    op("dve", lambda q, dg=dg: q.tensor_tensor(
                        finstage[:, 0:W2K], resid[:, dg * W2K:(dg + 1) * W2K],
                        sqF[:, 0:W2K], ALU.add),
                        reads=[f"resid{dg}", "sqFn"], writes=["finstage"])
                    for tt in range(16):
                        po = (tt % 4) * 512
                        op("pe", lambda q, tt=tt, po=po: q.transpose(
                            psum("B", po, po + 128), finstage[:, tt * 128:(tt + 1) * 128],
                            id128[:, :]),
                            reads=["finstage", "id128"], writes=pskeys("B", po, po + 128))
                        op("dve", lambda q, po=po: q.tensor_copy(
                            outst[:, 0:128], psum("B", po, po + 128)),
                            reads=pskeys("B", po, po + 128), writes=["outst"])
                        dma("sp", "out", io["out"][tt * 128:(tt + 1) * 128, dg * 128:(dg + 1) * 128],
                            outst[:, 0:128], reads=["outst"])

        if dbg_names:
            s.barrier()
            named = {
                "hbf": hbf, "x_s": x_s, "sz": sz, "delta": delta, "dx": dx,
                "yg": yg, "xdbl": xdbl, "xdbl_c": xdbl_c,
                "carry0": carry[0][:, :], "carry1": carry[1][:, :],
                "yacc": yacc,
                "s_t": s_t[0].bitcast(BF16), "b_t": b_t, "a_t2": a_t[0].bitcast(BF16),
            }
            if mstop >= 8:
                named["resid"] = resid
                named["mo"] = mo
            if mstop >= 9:
                named["cwT"] = cwT_sb[0:15, :]
                named["lg_em"] = lg_em[0:15, 0:2048]
                named["cwsc"] = cwsc[:, 0:2048]
                named["hsc"] = h_sc
                named["ws0"] = wslot[0]
                named["ws2"] = wslot[2]
            if mstop >= 9:
                named["acc"] = acc
            for i, nm in enumerate(dbg_names):
                src = named[nm]
                p, cols = src.shape[0], src.shape[1]
                dma("pool", "dbg", dbg[i][0:p, 0:cols], src[0:p, 0:cols])
        s.barrier()
    with nc.Block() as block:
        s.replay(block)
    es.close()
    return nc, io




def kernel(**inputs):
    from concourse.bass_utils import run_bass_kernel_spmd
    maps = host_prep(inputs)
    nc, io = build_program("full")
    res = run_bass_kernel_spmd(nc, maps, core_ids=list(range(NCORE)))
    outs = []
    for bb in range(B):
        rows = [np.asarray(res.results[4 * bb + q]["out"]) for q in range(4)]
        outs.append(np.concatenate(rows, axis=0))
    return np.stack(outs).astype(np.float32)


# version 06
# speedup vs baseline: 1.2124x; 1.2124x over previous
import sys
try:
    import concourse.bass
except ImportError:
    sys.path.insert(0, "/opt/trn_rl_repo")

"""Megakernel builder for mk_sandwich_d1 (Mamba-S6 + dense-MoE sandwich).

Sharding (8 cores = one TRN2 chip):
  Mamba : core c -> batch b=c//4, d_inner quarter q=c%4 (512 channels),
          full sequence (scan entirely on-core). x_dbl partials AllReduced
          per 1024-token block across the 4 cores of the batch; out_proj
          partials ReduceScattered to the FFN token owners.
  FFN   : core c -> 2048 tokens (quarter q of batch b); dense experts with
          top-3 routed weights; the pre-FFN LN is folded into the matmuls
          (rank-1 mean fixup; rstd applied after relu; biases pre-divided
          by rstd via a std-row bias matmul).
"""
import contextlib
import numpy as np
import ml_dtypes

import concourse.bass as bass
import concourse.mybir as mybir

F32 = mybir.dt.float32
BF16 = mybir.dt.bfloat16
AF = mybir.ActivationFunctionType
ALU = mybir.AluOpType
AX = mybir.AxisListType
bf = ml_dtypes.bfloat16

B, L, D = 2, 8192, 1024
DI, DS, DC, DTR = 2048, 16, 4, 64
E, H = 15, 1024
NCORE = 8
CH = DI // 4
TOK = L // 4
TBS = 1024
NTB = L // TBS
EPS = 1e-5
GROUPS = [[0, 1, 2, 3], [4, 5, 6, 7]]
ENGS = ("pe", "act", "dve", "pool", "sp")


class Sync:

    def __init__(self, nc, es):
        self.nc = nc
        self.eng = {}
        self.insts = {k: [] for k in ENGS}
        self.incs = {k: [] for k in ENGS}
        self.sem = {k: es.enter_context(nc.semaphore(f"sem_{k}")) for k in ENGS}
        self.clock = {a: {b: -1 for b in ENGS} for a in ENGS}
        self.dma_sems = {}
        self.dma_cnt = {}
        self.dma_clock = {a: {} for a in ENGS}
        self.cc_clock = {}
        self._cc = {}
        self.buf = {}
        self.es = es

    def set_queues(self, **eng_objs):
        self.eng = eng_objs

    def _dma_sem(self, tag):
        if tag not in self.dma_sems:
            self.dma_sems[tag] = self.es.enter_context(
                self.nc.semaphore(f"dma_{tag}"))
            self.dma_cnt[tag] = 0
        return self.dma_sems[tag]

    def _ensure_inc(self, En, i):
        incs = self.incs[En]
        for pos, k in enumerate(incs):
            if k >= i:
                return self.sem[En], pos + 1
        k = len(self.insts[En]) - 1
        while k >= 0 and (self.insts[En][k]["fn"] is None
                          or self.insts[En][k].get("dma")):
            k -= 1
        assert k >= i >= 0, f"dep on future inst {En}:{i} (have {k})"
        self.insts[En][k]["incs"].append((self.sem[En], 1))
        incs.append(k)
        return self.sem[En], len(incs)

    def _dep(self, eng, w, waits, dist_guard=2):
        kind, En, i = w
        if kind == "cc":
            ck = self.cc_clock.setdefault(eng, set())
            if En in ck:
                return
            waits.append((self._cc[En], 1))
            ck.add(En)
            return
        if kind == "dma":
            cnt = self.dma_cnt[En]
            if self.dma_clock[eng].get(En, 0) >= cnt:
                return
            waits.append((self._dma_sem(En), 16 * cnt))
            self.dma_clock[eng][En] = cnt
            return
        if En == eng:
            idx = len(self.insts[eng])
            if eng == "pe" or idx - i > dist_guard:
                return
        elif self.clock[eng][En] >= i:
            return
        sem, val = self._ensure_inc(En, i)
        waits.append((sem, val))
        if En != eng:
            self.clock[eng][En] = self.incs[En][val - 1]

    def _gather(self, eng, reads, writes, scalar_reads):
        waits = []
        for key in reads:
            ent = self.buf.get(key)
            if ent and ent["w"]:
                self._dep(eng, ent["w"], waits)
        for key in scalar_reads:
            ent = self.buf.get(key)
            if ent and ent["w"]:
                self._dep(eng, ent["w"], waits, dist_guard=4)
        for key in writes:
            ent = self.buf.get(key)
            if ent:
                if ent["w"]:
                    self._dep(eng, ent["w"], waits)
                for r in ent["r"]:
                    self._dep(eng, r, waits)
        ded = {}
        for sem, val in waits:
            k = id(sem)
            if k not in ded or ded[k][1] < val:
                ded[k] = (sem, val)
        return list(ded.values())

    def _record(self, who, reads, writes, scalar_reads=()):
        for key in writes:
            self.buf[key] = {"w": who, "r": []}
        for key in list(reads) + list(scalar_reads):
            ent = self.buf.setdefault(key, {"w": None, "r": []})
            ent["r"].append(who)

    def op(self, eng, fn, reads=(), writes=(), scalar_reads=()):
        waits = self._gather(eng, reads, writes, scalar_reads)
        rec = {"waits": waits, "fn": fn, "incs": []}
        idx = len(self.insts[eng])
        self.insts[eng].append(rec)
        self._record(("eng", eng, idx), reads, writes, scalar_reads)
        return rec

    def dma(self, queue, tag, out, in_, reads=(), writes=()):
        waits = self._gather(queue, reads, writes, ())
        sem = self._dma_sem(tag)
        rec = {"waits": waits,
               "fn": (lambda q, out=out, in_=in_: q.dma_start(out=out, in_=in_)),
               "incs": [(sem, 16)], "dma": True}
        self.insts[queue].append(rec)
        self.dma_cnt[tag] += 1
        self._record(("dma", tag, self.dma_cnt[tag]), reads, writes)
        return rec

    def collective(self, fn, tag, reads=(), writes=()):
        waits = self._gather("pool", reads, writes, ())
        sem = self.es.enter_context(self.nc.semaphore(f"cc_{tag}"))
        self._cc[tag] = sem
        rec = {"waits": waits, "fn": fn, "incs": [(sem, None)], "dma": True}
        self.insts["pool"].append(rec)
        self._record(("cc", tag, 1), reads, writes)
        return rec

    def barrier(self):
        for En in ENGS:
            eng_insts = [i for i, r in enumerate(self.insts[En])
                         if r["fn"] is not None and not r.get("dma")]
            if eng_insts:
                self._ensure_inc(En, eng_insts[-1])
        for eng in ENGS:
            waits = []
            for En in ENGS:
                if En == eng:
                    continue
                if self.incs[En]:
                    waits.append((self.sem[En], len(self.incs[En])))
                    self.clock[eng][En] = self.incs[En][-1]
            for tag, cnt in list(self.dma_cnt.items()):
                if cnt > 0 and self.dma_clock[eng].get(tag, 0) < cnt:
                    waits.append((self.dma_sems[tag], 16 * cnt))
                    self.dma_clock[eng][tag] = cnt
            for tag in self._cc:
                ck = self.cc_clock.setdefault(eng, set())
                if tag not in ck:
                    waits.append((self._cc[tag], 1))
                    ck.add(tag)
            self.insts[eng].append({"waits": waits, "fn": None, "incs": []})

    def replay(self, block):
        def runner(eng):
            def body(q):
                for rec in self.insts[eng]:
                    for sem, val in rec["waits"]:
                        q.wait_ge(sem, val)
                    if rec["fn"] is None:
                        continue
                    inst = rec["fn"](q)
                    for sem, amt in rec["incs"]:
                        if amt is None:
                            inst.then_inc(sem)
                        else:
                            inst.then_inc(sem, amt)
            return body
        block.tensor(runner("pe"))
        block.scalar(runner("act"))
        block.vector(runner("dve"))
        block.gpsimd(runner("pool"))
        block.sync(runner("sp"))


class Arena:
    def __init__(self, nc, es, name, dt, cols):
        self.t = es.enter_context(nc.sbuf_tensor(name, [128, cols], dt))
        self.off = 0
        self.cols = cols
        self.marks = []

    def alloc(self, cols):
        off = self.off
        self.off += cols
        assert self.off <= self.cols, f"arena overflow {self.off}>{self.cols}"
        return self.t[0:128, off:off + cols]

    def push(self):
        self.marks.append(self.off)

    def pop(self):
        self.off = self.marks.pop()


def host_prep(inputs):
    h = np.asarray(inputs["L0_attn_hidden_states"])
    W_in = np.asarray(inputs["L0_attn_W_in"])
    C = np.ascontiguousarray
    sel16 = np.zeros((16, 16 * 128), dtype=bf)
    for n in range(16):
        sel16[n, n * 128:(n + 1) * 128] = 1.0
    id128 = np.eye(128, dtype=np.float32)
    ones_row = np.ones((1, 512), dtype=bf)
    hT = [C(h[b].T) for b in range(B)]
    maps = []
    for c in range(NCORE):
        b, q = c // 4, c % 4
        chs = np.arange(CH * q, CH * (q + 1))
        Wsl = np.concatenate([W_in[2 * chs], W_in[2 * chs + 1]], axis=0)
        m = {
            "hT": hT[b],
            "hres": C(h[b, TOK * q:TOK * (q + 1), :].T),
            "w_inT": C(Wsl.T),
            "w_outT": C(np.asarray(inputs["L0_attn_W_out"])[:, chs].T),
            "xpwT": C(np.asarray(inputs["L0_attn_x_proj_w"])[:, chs].T),
            "dtwT": C(np.asarray(inputs["L0_attn_dt_w"])[chs, :].T),
            "alog": C(np.asarray(inputs["L0_attn_A_log"])[chs]),
            "convw": C(np.asarray(inputs["L0_attn_conv_w"])[chs]),
            "convb": C(np.asarray(inputs["L0_attn_conv_b"])[chs]),
            "dtb": C(np.asarray(inputs["L0_attn_dt_b"])[chs]),
            "dvec": C(np.asarray(inputs["L0_attn_D"])[chs]),
            "rw": np.asarray(inputs["L0_ffn_router_W"]),
            "rb": np.asarray(inputs["L0_ffn_router_b"]).reshape(1, E),
            "sw1": np.asarray(inputs["L0_ffn_sW1"]),
            "sw2": np.asarray(inputs["L0_ffn_sW2"]),
            "ew1": np.asarray(inputs["L0_ffn_eW1"]),
            "ew2": np.asarray(inputs["L0_ffn_eW2"]),
            "sb1r": np.asarray(inputs["L0_ffn_sb1"]).reshape(1, H),
            "sb2r": np.asarray(inputs["L0_ffn_sb2"]).reshape(1, D),
            "eb1r": np.asarray(inputs["L0_ffn_eb1"]).reshape(1, E * H),
            "eb2r": np.asarray(inputs["L0_ffn_eb2"]).reshape(1, E * D),
            "sel16": sel16, "id128": id128, "ones_row": ones_row,
        }
        maps.append(m)
    return maps


def declare_io(nc):
    d = {}
    def di(name, shape, dt=F32):
        d[name] = nc.dram_tensor(name, list(shape), dt, kind="ExternalInput").ap()
    di("hT", (D, L)); di("hres", (D, TOK))
    di("w_inT", (D, 2 * CH)); di("w_outT", (CH, D))
    di("xpwT", (CH, 96)); di("dtwT", (DTR, CH))
    di("alog", (CH, DS)); di("convw", (CH, DC))
    di("convb", (CH,)); di("dtb", (CH,)); di("dvec", (CH,))
    di("rw", (D, E)); di("rb", (1, E))
    di("sw1", (D, H)); di("sw2", (H, D))
    di("ew1", (E, D, H)); di("ew2", (E, H, D))
    di("sb1r", (1, H)); di("sb2r", (1, D))
    di("eb1r", (1, E * H)); di("eb2r", (1, E * D))
    di("sel16", (16, 16 * 128), BF16)
    di("id128", (128, 128))
    di("ones_row", (1, 512), BF16)
    d["out"] = nc.dram_tensor("out", [TOK, D], F32, kind="ExternalOutput").ap()
    for tb in range(NTB):
        d[f"ccx_in{tb}"] = nc.dram_tensor(f"ccx_in{tb}", [96, TBS], BF16, kind="Internal").ap()
        d[f"ccx_out{tb}"] = nc.dram_tensor(f"ccx_out{tb}", [96, TBS], BF16, kind="Internal").ap()
    d["ccm_in"] = nc.dram_tensor("ccm_in", [NTB * D, TBS], BF16, kind="Internal").ap()
    d["wbf"] = nc.dram_tensor("wbf", [32, D, H], BF16, kind="Internal").ap()
    d["ebbf"] = nc.dram_tensor("ebbf", [1, 2 * E * H], BF16, kind="Internal").ap()
    d["ccm_out"] = nc.dram_tensor("ccm_out", [2 * D, TBS], BF16, kind="Internal").ap()
    return d


def build_program(upto="full", dbg_names=()):
    nc = bass.Bass("TRN2", target_bir_lowering=False)
    io = declare_io(nc)
    es = contextlib.ExitStack()
    s = Sync(nc, es)

    dbg = []
    for i in range(len(dbg_names)):
        dbg.append(nc.dram_tensor(f"dbg{i}", [128, 16 * 1024], F32, kind="ExternalOutput").ap())

    sb_ = lambda n, shape, dt: es.enter_context(nc.sbuf_tensor(n, shape, dt))

    xpwT = sb_("sb_xpwT", [128, 4 * 96], BF16)
    dtwT = sb_("sb_dtwT", [64, 4 * 128], BF16)
    A_sb = sb_("A_sb", [128, 4 * 16], F32)
    convw = sb_("sb_convw", [128, 4 * 4], F32)
    convb = sb_("sb_convb", [128, 4], F32)
    dtb = sb_("sb_dtb", [128, 4], F32)
    dvec = sb_("sb_dvec", [128, 4], F32)
    ndtb = sb_("sb_ndtb", [128, 4], F32)
    rw_sb = sb_("rw_sb", [128, 8 * E], BF16)
    sel16 = sb_("sb_sel16", [16, 16 * 128], BF16)
    id128 = sb_("sb_id128", [128, 128], F32)
    id128b = sb_("sb_id128b", [128, 128], BF16)
    ones_bf = sb_("ones_bf", [128, 512], BF16)
    onescol_bf = sb_("onescol_bf", [128, 1], BF16)
    sb1row = sb_("sb1row", [1, H], BF16)
    sb2row = sb_("sb2row", [1, D], BF16)
    rbrow = sb_("rbrow", [1, E], BF16)
    ebst = [sb_("ebst0", [1, 2 * H], BF16), sb_("ebst1", [1, 2 * H], BF16)]
    negw1 = sb_("negw1", [1, 2 * CH], BF16)
    xdbl = sb_("xdbl", [80, TBS], BF16)
    xdbl_c = sb_("xdbl_c", [16, TBS], BF16)
    xdbl_b = sb_("xdbl_b", [16, TBS], BF16)
    carry = [sb_("carry0", [128, 64], F32), sb_("carry1", [128, 64], F32)]
    stat = sb_("stat", [128, 2048], F32)
    rowbf = sb_("rowbf", [128, 4096], BF16)
    wrow = sb_("wrow", [1, 1040], BF16)
    cwrow = sb_("cwrow", [1, 2048], BF16)
    cwT_sb = sb_("cwT_sb", [15, 2048], BF16)
    cpool = sb_("cpool", [1, 4], F32)
    nc.const_aps.register(F32, EPS, cpool[0:1, 0:1])

    AB = Arena(nc, es, "AB", BF16, 69696)
    AFR = Arena(nc, es, "AFR", F32, 4320)
    psA_t = es.enter_context(nc.psum_tensor("psA", [128, 2048], F32))
    psB_t = es.enter_context(nc.psum_tensor("psB", [128, 2048], F32))

    if True:
        op, dma, coll = s.op, s.dma, s.collective

        def psum(arena, lo, hi, rows=128, base=0):
            t = psA_t if arena == "A" else psB_t
            return t[base:base + rows, lo:hi]

        def pskeys(arena, lo, hi):
            return [f"ps{arena}{i}" for i in range(lo // 512, (hi + 511) // 512)]

        dma("pool", "prep2", xpwT.ap().rearrange("p (g e) -> p g e", g=4),
            io["xpwT"].rearrange("(g p) e -> p g e", p=128), writes=["xpwT"])
        dma("pool", "prep2", dtwT.ap().rearrange("r (g c) -> r g c", g=4),
            io["dtwT"].rearrange("r (g c) -> r g c", g=4), writes=["dtwT"])
        dma("pool", "prep2", rw_sb.ap().rearrange("p (g e) -> p g e", g=8),
            io["rw"].rearrange("(g p) e -> p g e", p=128), writes=["rw_sb"])
        dma("sp", "prep", sel16[0:16, :], io["sel16"][:, :], writes=["sel16"])
        dma("sp", "prep", id128[:, :], io["id128"][:, :], writes=["id128"])
        op("act", lambda q: q.activation(id128b[:, :], id128[:, :], AF.Copy),
           reads=["id128"], writes=["id128b"])
        op("dve", lambda q: q.memset(ones_bf[:, :], 1.0), writes=["ones_bf"])
        for g in range(4):
            dma("sp", "prep", convw[:, g * 4:(g + 1) * 4],
                io["convw"][g * 128:(g + 1) * 128, :], writes=["convw"])
            dma("sp", "prep", convb[:, g:g + 1],
                io["convb"][g * 128:(g + 1) * 128].unsqueeze(1), writes=["convb"])
            dma("sp", "prep", dtb[:, g:g + 1],
                io["dtb"][g * 128:(g + 1) * 128].unsqueeze(1), writes=["dtb"])
            dma("sp", "prep", dvec[:, g:g + 1],
                io["dvec"][g * 128:(g + 1) * 128].unsqueeze(1), writes=["dvec"])
        dma("pool", "prep2", sb1row[0:1, :], io["sb1r"][:, :], writes=["sb1row"])
        dma("pool", "prep2", sb2row[0:1, :], io["sb2r"][:, :], writes=["sb2row"])
        dma("pool", "prep2", rbrow[0:1, :], io["rb"][:, :], writes=["rbrow"])
        op("dve", lambda q: q.memset(onescol_bf[:, :], 1.0), writes=["onescol_bf"])
        for k in range(32):
            li_k, isw1 = k // 2, (k % 2 == 0)
            if li_k == 0:
                srcw = io["sw1"] if isw1 else io["sw2"]
            else:
                srcw = (io["ew1"] if isw1 else io["ew2"])[li_k - 1, :, :]
            dma("pool", "wcast", io["wbf"][k, :, :], srcw, writes=["wbf"])
        dma("pool", "wcast", io["ebbf"][:, 0:E * H], io["eb1r"][:, :], writes=["ebbf"])
        dma("pool", "wcast", io["ebbf"][:, E * H:2 * E * H], io["eb2r"][:, :], writes=["ebbf"])
        op("dve", lambda q: q.memset(cpool[0:1, 0:1], EPS), writes=["cpool"])
        op("dve", lambda q: q.memset(carry[0][:, :], 0.0), writes=["carry0"])
        op("dve", lambda q: q.memset(carry[1][:, :], 0.0), writes=["carry1"])
        for _ln in range(2):
            for _pp in range(2):
                op("dve", lambda q, _ln=_ln, _pp=_pp: q.memset(a_t[_ln][_pp][:, 0:1], 0.0),
                   writes=[f"a_t{_ln}_{_pp}"])
        op("dve", lambda q: q.tensor_single_scalar(ndtb[:, :], dtb[:, :], -1.0, ALU.mult),
           reads=["dtb"], writes=["ndtb"])

        AB.push()
        AFR.push()
        winT = AB.alloc(8 * 1024)
        woutT = AB.alloc(4 * 1024)
        hbf = AB.alloc(8 * TBS)
        Brep = [AB.alloc(TBS) for _ in range(2)]
        Crep = [AB.alloc(TBS) for _ in range(2)]
        b_t = AB.alloc(TBS + 2)
        s_t = [AB.alloc(2 * TBS + 4).bitcast(F32) for _ in range(2)]
        w_t = AB.alloc(TBS)
        sqM = AB.alloc(TBS)
        tmpM = Brep[0]
        ostage = sqM
        x_raw = [AB.alloc(TBS + 4) for _ in range(4)]
        x_s = AB.alloc(4 * TBS)
        sz = AB.alloc(4 * TBS)
        delta = AB.alloc(4 * TBS)
        dx = AB.alloc(4 * TBS)
        yg = AB.alloc(4 * TBS)
        w_t2 = AB.alloc(TBS)
        a_t = [[AB.alloc(2 * TBS + 4).bitcast(F32) for _ in range(2)] for _ in range(2)]
        yacc = AB.alloc(4 * TBS)
        bcast_bf = w_t
        stage = AFR.alloc(TBS)
        stage2 = AFR.alloc(TBS)
        ytmp = AFR.alloc(TBS)

        dma("pool", "winT", winT.rearrange("p (g e) -> p g e", g=8),
            io["w_inT"].rearrange("(g p) e -> p g e", p=128), writes=["winT"])
        dma("pool", "woutT", woutT.rearrange("p (g m) -> p g m", g=4),
            io["w_outT"].rearrange("(g p) m -> p g m", p=128), writes=["woutT"])
        dma("sp", "alog", stage[:, 0:64].rearrange("p (g n) -> p g n", g=4),
            io["alog"].rearrange("(g p) n -> p g n", p=128), writes=["stage"])
        op("act", lambda q: q.activation(A_sb[:, :], stage[:, 0:64], AF.Exp),
           reads=["stage"], writes=["A_sb"])
        op("dve", lambda q: q.tensor_single_scalar(A_sb[:, :], A_sb[:, :], -1.0, ALU.mult),
           reads=["A_sb"], writes=["A_sb"])

        for t5 in range(2):
            lo, hi = t5 * 512, (t5 + 1) * 512
            for g in range(8):
                op("pe", lambda q, g=g, lo=lo, hi=hi: q.matmul(
                    psum("A", lo, hi, rows=1), onescol_bf[:, 0:1],
                    winT[:, g * 1024 + lo:g * 1024 + hi],
                    start=(g == 0), stop=(g == 7)),
                    reads=["winT", "onescol_bf"], writes=pskeys("A", lo, hi))
        op("dve", lambda q: q.tensor_single_scalar(
            negw1[0:1, :], psum("A", 0, 1024, rows=1), -1.0, ALU.mult),
            reads=pskeys("A", 0, 1024), writes=["negw1"])

        mstop = {"inproj": 0, "conv": 1, "xproj": 2, "delta": 3, "scan": 5,
                 "outproj": 6, "mamba": 7, "resid": 8, "router": 9,
                 "experts": 10}.get(upto, 99)
        import os
        ntb_run = NTB if mstop >= 7 else 1
        if os.environ.get("NTB_OVERRIDE"):
            ntb_run = int(os.environ["NTB_OVERRIDE"])

        gidx = 0
        for tb in range(ntb_run):
            t0 = tb * TBS
            dma("pool", "hbf", hbf.rearrange("p (g t) -> p g t", g=8),
                io["hT"][:, t0:t0 + TBS].rearrange("(g p) t -> p g t", p=128),
                writes=["hbf"])
            for dg in range(8):
                op("act", lambda q, dg=dg: q.activation(
                    sqM[:, 0:TBS], hbf[:, dg * TBS:(dg + 1) * TBS], AF.Square),
                    reads=["hbf"], writes=["sqM", "ostage"])
                for t5 in range(2):
                    lo, hi = t5 * 512, (t5 + 1) * 512
                    op("pe", lambda q, dg=dg, lo=lo, hi=hi: q.matmul(
                        psum("A", lo, hi, rows=1), onescol_bf[:, 0:1],
                        hbf[:, dg * TBS + lo:dg * TBS + hi],
                        start=(dg == 0), stop=(dg == 7)),
                        reads=["hbf", "onescol_bf"], writes=pskeys("A", lo, hi))
                    op("pe", lambda q, dg=dg, lo=lo, hi=hi: q.matmul(
                        psum("B", lo, hi, rows=1), onescol_bf[:, 0:1],
                        sqM[:, lo:hi], start=(dg == 0), stop=(dg == 7)),
                        reads=["sqM", "onescol_bf"], writes=pskeys("B", lo, hi))
            op("dve", lambda q: q.tensor_single_scalar(
                stat[0:1, 0:TBS], psum("A", 0, TBS, rows=1), 1.0 / D, ALU.mult),
                reads=pskeys("A", 0, TBS), writes=["stat_m1"])
            op("dve", lambda q: q.tensor_single_scalar(
                stage[0:1, 0:TBS], psum("B", 0, TBS, rows=1), 1.0 / D, ALU.mult),
                reads=pskeys("B", 0, TBS), writes=["stage"])
            op("dve", lambda q: q.tensor_tensor(
                stage2[0:1, 0:TBS], stat[0:1, 0:TBS], stat[0:1, 0:TBS], ALU.mult),
                reads=["stat_m1"], writes=["stage2"])
            op("dve", lambda q: q.tensor_tensor(
                stage2[0:1, 0:TBS], stage[0:1, 0:TBS], stage2[0:1, 0:TBS], ALU.subtract),
                reads=["stage", "stage2"], writes=["stage2"])
            op("act", lambda q: q.activation(
                stage2[0:1, 0:TBS], stage2[0:1, 0:TBS], AF.Ln, bias=EPS),
                reads=["stage2", "cpool"], writes=["stage2"])
            op("act", lambda q: q.activation(
                stat[32:33, 0:TBS], stage2[0:1, 0:TBS], AF.Exp, scale=-0.5),
                reads=["stage2"], writes=["stat_r1"])
            op("dve", lambda q: q.tensor_copy(rowbf[0:1, 0:TBS], stat[0:1, 0:TBS]),
               reads=["stat_m1"], writes=["rowbf_m1"])
            op("act", lambda q: q.activation(
                rowbf[32:33, 0:TBS], stat[32:33, 0:TBS], AF.Copy),
                reads=["stat_r1"], writes=["rowbf_r1"])
            for t5 in range(2):
                lo, hi = t5 * 512, (t5 + 1) * 512
                op("pe", lambda q, lo=lo, hi=hi: q.matmul(
                    psum("B", lo, hi), ones_bf[32:33, 0:128],
                    rowbf[32:33, lo:hi], start=True, stop=True),
                    reads=["rowbf_r1", "ones_bf"], writes=pskeys("B", lo, hi))
            op("dve", lambda q: q.tensor_copy(bcast_bf[:, 0:TBS], psum("B", 0, TBS)),
               reads=pskeys("B", 0, TBS), writes=["rstd1b", "w_t"])

            for ec in range(8):
                pa = "A" if ec % 2 == 0 else "B"
                for t5 in range(2):
                    lo, hi = t5 * 512, (t5 + 1) * 512
                    for dg in range(8):
                        op("pe", lambda q, ec=ec, dg=dg, lo=lo, hi=hi, pa=pa: q.matmul(
                            psum(pa, lo, hi),
                            winT[:, dg * 1024 + ec * 128:dg * 1024 + (ec + 1) * 128],
                            hbf[:, dg * TBS + lo:dg * TBS + hi],
                            start=(dg == 0), stop=False),
                            reads=["winT", "hbf"], writes=pskeys(pa, lo, hi))
                    op("pe", lambda q, ec=ec, lo=lo, hi=hi, pa=pa: q.matmul(
                        psum(pa, lo, hi),
                        negw1[0:1, ec * 128:(ec + 1) * 128],
                        rowbf[0:1, lo:hi],
                        start=False, stop=True),
                        reads=["negw1", "rowbf_m1"], writes=pskeys(pa, lo, hi))
                if ec < 4:
                    chg = ec
                    if tb == 0:
                        op("dve", lambda q, chg=chg: q.memset(x_raw[chg][:, 0:3], 0.0),
                           writes=[f"x_raw{chg}"])
                    else:
                        op("dve", lambda q, chg=chg: q.tensor_copy(
                            x_raw[chg][:, 0:3], x_raw[chg][:, TBS:TBS + 3]),
                            reads=[f"x_raw{chg}"], writes=[f"x_raw{chg}"])
                    op("dve", lambda q, chg=chg, pa=pa: q.tensor_tensor(
                        x_raw[chg][:, 3:3 + TBS], psum(pa, 0, TBS),
                        bcast_bf[:, 0:TBS], ALU.mult),
                        reads=pskeys(pa, 0, TBS) + ["rstd1b"], writes=[f"x_raw{chg}"])
                else:
                    chg = ec - 4
                    op("dve", lambda q, pa=pa: q.tensor_tensor(
                        stage[:, 0:TBS], psum(pa, 0, TBS), bcast_bf[:, 0:TBS], ALU.mult),
                        reads=pskeys(pa, 0, TBS) + ["rstd1b"], writes=["stage"])
                    op("act", lambda q, chg=chg: q.activation(
                        sz[:, chg * TBS:(chg + 1) * TBS], stage[:, 0:TBS], AF.Silu),
                        reads=["stage"], writes=[f"sz{chg}"])
            if mstop <= 0:
                break

            for chg in range(4):
                op("dve", lambda q, chg=chg: q.tensor_single_scalar(
                    stage2[:, 0:TBS], x_raw[chg][:, 0:TBS],
                    convw[:, chg * 4:chg * 4 + 1], ALU.mult),
                    reads=[f"x_raw{chg}"], writes=["stage2"], scalar_reads=["convw"])
                for k in range(1, 4):
                    op("dve", lambda q, chg=chg, k=k: q.scalar_tensor_tensor(
                        stage2[:, 0:TBS], x_raw[chg][:, k:k + TBS],
                        convw[:, chg * 4 + k:chg * 4 + k + 1], stage2[:, 0:TBS],
                        ALU.mult, ALU.add),
                        reads=[f"x_raw{chg}", "stage2"], writes=["stage2"],
                        scalar_reads=["convw"])
                op("act", lambda q, chg=chg: q.activation(
                    x_s[:, chg * TBS:(chg + 1) * TBS], stage2[:, 0:TBS], AF.Silu,
                    bias=convb[:, chg:chg + 1]),
                    reads=["stage2", "convb"], writes=[f"x_s{chg}"])
            if mstop <= 1:
                break

            for t5 in range(2):
                lo, hi = t5 * 512, (t5 + 1) * 512
                for chg in range(4):
                    op("pe", lambda q, chg=chg, lo=lo, hi=hi: q.matmul(
                        psum("A", lo, hi, rows=80),
                        xpwT[:, chg * 96:chg * 96 + 80],
                        x_s[:, chg * TBS + lo:chg * TBS + hi],
                        start=(chg == 0), stop=(chg == 3)),
                        reads=["xpwT", f"x_s{chg}"], writes=pskeys("A", lo, hi))
                    op("pe", lambda q, chg=chg, lo=lo, hi=hi: q.matmul(
                        psum("B", lo, hi, rows=16),
                        xpwT[:, chg * 96 + 80:chg * 96 + 96],
                        x_s[:, chg * TBS + lo:chg * TBS + hi],
                        start=(chg == 0), stop=(chg == 3)),
                        reads=["xpwT", f"x_s{chg}"], writes=pskeys("B", lo, hi))
            op("dve", lambda q: q.tensor_copy(xdbl[0:80, :], psum("A", 0, TBS, rows=80)),
               reads=pskeys("A", 0, TBS), writes=["xdbl"])
            op("dve", lambda q: q.tensor_copy(xdbl_c[0:16, :], psum("B", 0, TBS, rows=16)),
               reads=pskeys("B", 0, TBS), writes=["xdbl_c"])
            dma("sp", f"cxi{tb}", io[f"ccx_in{tb}"][0:80, :], xdbl[0:80, :], reads=["xdbl"])
            dma("sp", f"cxi{tb}", io[f"ccx_in{tb}"][80:96, :], xdbl_c[0:16, :], reads=["xdbl_c"])
            s.buf[f"ccxi{tb}"] = {"w": ("dma", f"cxi{tb}", 2), "r": []}
            coll(lambda q, tb=tb: q.collective_compute(
                "AllReduce", ALU.add, replica_groups=GROUPS,
                ins=[io[f"ccx_in{tb}"].opt()], outs=[io[f"ccx_out{tb}"].opt()]),
                f"arx{tb}", reads=[f"ccxi{tb}"], writes=[f"ccxo{tb}"])
            dma("sp", f"cxo{tb}", xdbl[0:64, :], io[f"ccx_out{tb}"][0:64, :],
                reads=[f"ccxo{tb}"], writes=["xdbl"])
            dma("sp", f"cxo{tb}", xdbl_b[0:16, :], io[f"ccx_out{tb}"][64:80, :],
                reads=[f"ccxo{tb}"], writes=["xdbl_b"])
            dma("sp", f"cxo{tb}", xdbl_c[0:16, :], io[f"ccx_out{tb}"][80:96, :],
                reads=[f"ccxo{tb}"], writes=["xdbl_c"])
            if mstop <= 2:
                break

            for chg in range(4):
                pa = "A" if chg % 2 == 0 else "B"
                for t5 in range(2):
                    lo, hi = t5 * 512, (t5 + 1) * 512
                    op("pe", lambda q, chg=chg, lo=lo, hi=hi, pa=pa: q.matmul(
                        psum(pa, lo, hi), dtwT[0:64, chg * 128:(chg + 1) * 128],
                        xdbl[0:64, lo:hi], start=True, stop=True),
                        reads=["dtwT", "xdbl"], writes=pskeys(pa, lo, hi))
                op("act", lambda q, chg=chg, pa=pa: q.activation(
                    stage[:, 0:TBS], psum(pa, 0, TBS), AF.Relu, bias=dtb[:, chg:chg + 1]),
                    reads=pskeys(pa, 0, TBS) + ["dtb"], writes=["stage"])
                op("act", lambda q, chg=chg, pa=pa: q.activation(
                    stage2[:, 0:TBS], psum(pa, 0, TBS), AF.Relu,
                    bias=ndtb[:, chg:chg + 1], scale=-1.0),
                    reads=pskeys(pa, 0, TBS) + ["ndtb"], writes=["stage2"])
                op("dve", lambda q: q.tensor_tensor(
                    sqM[:, 0:TBS], stage[:, 0:TBS], stage2[:, 0:TBS], ALU.add),
                    reads=["stage", "stage2"], writes=["sqM", "ostage"])
                op("act", lambda q: q.activation(tmpM[:, 0:TBS], sqM[:, 0:TBS], AF.Exp, scale=-1.0),
                   reads=["sqM"], writes=["tmpM", "Brep0"])
                op("act", lambda q: q.activation(sqM[:, 0:TBS], tmpM[:, 0:TBS], AF.Ln, bias=1.0),
                   reads=["tmpM"], writes=["sqM", "ostage"])
                op("dve", lambda q, chg=chg: q.tensor_tensor(
                    delta[:, chg * TBS:(chg + 1) * TBS], stage[:, 0:TBS], sqM[:, 0:TBS], ALU.add),
                    reads=["stage", "sqM"], writes=[f"delta{chg}"])
                op("dve", lambda q, chg=chg: q.tensor_tensor(
                    dx[:, chg * TBS:(chg + 1) * TBS],
                    delta[:, chg * TBS:(chg + 1) * TBS],
                    x_s[:, chg * TBS:(chg + 1) * TBS], ALU.mult),
                    reads=[f"delta{chg}", f"x_s{chg}"], writes=[f"dx{chg}"])
            if mstop <= 3:
                break

            cprev, ccur = carry[gidx % 2], carry[(gidx + 1) % 2]
            kprev, kcur = f"carry{gidx % 2}", f"carry{(gidx + 1) % 2}"
            for chg in range(4):
                op("dve", lambda q, chg=chg: q.tensor_single_scalar(
                    yacc[:, chg * TBS:(chg + 1) * TBS], x_s[:, chg * TBS:(chg + 1) * TBS],
                    dvec[:, chg:chg + 1], ALU.mult),
                    reads=[f"x_s{chg}"], writes=[f"yacc{chg}"], scalar_reads=["dvec"])

            def bc_mms(n):
                for t5 in range(2):
                    lo, hi = t5 * 512, (t5 + 1) * 512
                    op("pe", lambda q, n=n, lo=lo, hi=hi: q.matmul(
                        psum("A", lo, hi), sel16[0:16, n * 128:(n + 1) * 128],
                        xdbl_b[0:16, lo:hi], start=True, stop=True),
                        reads=["sel16", "xdbl_b"], writes=pskeys("A", lo, hi))
                    op("pe", lambda q, n=n, lo=lo, hi=hi: q.matmul(
                        psum("B", lo, hi), sel16[0:16, n * 128:(n + 1) * 128],
                        xdbl_c[0:16, lo:hi], start=True, stop=True),
                        reads=["sel16", "xdbl_c"], writes=pskeys("B", lo, hi))

            def emit_reps(n):
                pn = n % 2
                op("act", lambda q, pn=pn: q.activation(
                    Brep[pn][:, :], psum("A", 0, TBS), AF.Copy),
                    reads=pskeys("A", 0, TBS), writes=[f"Brep{pn}"])
                op("act", lambda q, pn=pn: q.activation(
                    Crep[pn][:, :], psum("B", 0, TBS), AF.Copy),
                    reads=pskeys("B", 0, TBS), writes=[f"Crep{pn}"])

            def emit_a(pr, n):
                pn = n % 2
                for lane in range(2):
                    chg = 2 * pr + lane
                    op("act", lambda q, chg=chg, n=n, lane=lane, pn=pn: q.activation(
                        a_t[lane][pn][:, 1:TBS + 1],
                        delta[:, chg * TBS:(chg + 1) * TBS], AF.Exp,
                        scale=A_sb[:, chg * 16 + n:chg * 16 + n + 1]),
                        reads=[f"delta{chg}"], writes=[f"a_t{lane}_{pn}"],
                        scalar_reads=["A_sb"])

            for pr in range(2):
                for chg in (2 * pr, 2 * pr + 1):
                    pa = "A" if chg % 2 == 0 else "B"
                    for t5 in range(2):
                        lo, hi = t5 * 512, (t5 + 1) * 512
                        op("pe", lambda q, chg=chg, pa=pa, lo=lo, hi=hi: q.matmul(
                            psum(pa, 1024 + lo, 1024 + hi), id128b[:, :],
                            yacc[:, chg * TBS + lo:chg * TBS + hi],
                            start=True, stop=False),
                            reads=[f"yacc{chg}", "id128b"],
                            writes=pskeys(pa, 1024 + lo, 1024 + hi))
                bc_mms(0)
                emit_a(pr, 0)
                emit_reps(0)
                bc_mms(1)
                for n in range(16):
                    pn = n % 2
                    if n < 15:
                        emit_a(pr, n + 1)
                        emit_reps(n + 1)
                    if n < 14:
                        bc_mms(n + 2)
                    for chg in (2 * pr, 2 * pr + 1):
                        col = chg * 16 + n
                        cp_ = chg % 2
                        st, stk = s_t[cp_], f"s_t{cp_}"
                        op("dve", lambda q, col=col, cprev=cprev: q.tensor_copy(
                            b_t[:, 0:1], cprev[:, col:col + 1]),
                            reads=[kprev], writes=["b_t"])
                        op("dve", lambda q, chg=chg, pn=pn: q.tensor_tensor(
                            b_t[:, 1:TBS + 1], dx[:, chg * TBS:(chg + 1) * TBS],
                            Brep[pn][:, :], ALU.mult),
                            reads=[f"dx{chg}", f"Brep{pn}"], writes=["b_t"])
                        init, initreads = cprev[:, col:col + 1], [kprev]
                        op("dve", lambda q, init=init, st=st, cp_=cp_, pn=pn: q.tensor_tensor_scan(
                            st[:, 0:TBS + 1], a_t[cp_][pn][:, 0:TBS + 1], b_t[:, 0:TBS + 1],
                            init, ALU.mult, ALU.add),
                            reads=[f"a_t{cp_}_{pn}", "b_t"], writes=[stk], scalar_reads=initreads)
                        wdst, wkey = (w_t, "w_t") if cp_ == 0 else (w_t2, "w_t2")
                        wk2 = [wkey, "rstd1b"] if wkey == "w_t" else [wkey]
                        op("pool", lambda q, wdst=wdst, st=st, pn=pn: q.tensor_tensor(
                            wdst[:, :], st[:, 1:TBS + 1], Crep[pn][:, :], ALU.mult),
                            reads=[stk, f"Crep{pn}"], writes=wk2)
                        op("pool", lambda q, col=col, ccur=ccur, st=st: q.tensor_copy(
                            ccur[:, col:col + 1], st[:, TBS:TBS + 1]),
                            reads=[stk], writes=[kcur])
                        pa = "A" if cp_ == 0 else "B"
                        for t5 in range(2):
                            lo, hi = t5 * 512, (t5 + 1) * 512
                            op("pe", lambda q, pa=pa, lo=lo, hi=hi, wdst=wdst, n=n: q.matmul(
                                psum(pa, 1024 + lo, 1024 + hi), id128b[:, :],
                                wdst[:, lo:hi], start=False, stop=(n == 15)),
                                reads=[wkey, "id128b"],
                                writes=pskeys(pa, 1024 + lo, 1024 + hi))
                for chg in (2 * pr, 2 * pr + 1):
                    pa = "A" if chg % 2 == 0 else "B"
                    op("dve", lambda q, chg=chg, pa=pa: q.tensor_tensor(
                        yg[:, chg * TBS:(chg + 1) * TBS], psum(pa, 1024, 2048),
                        sz[:, chg * TBS:(chg + 1) * TBS], ALU.mult),
                        reads=pskeys(pa, 1024, 2048) + [f"sz{chg}"], writes=[f"yg{chg}"])
            gidx += 1
            if mstop <= 5:
                break

            for mg in range(8):
                pa = "A" if mg % 2 == 0 else "B"
                for t5 in range(2):
                    lo, hi = t5 * 512, (t5 + 1) * 512
                    for chg in range(4):
                        op("pe", lambda q, mg=mg, chg=chg, lo=lo, hi=hi, pa=pa: q.matmul(
                            psum(pa, lo, hi),
                            woutT[:, chg * 1024 + mg * 128:chg * 1024 + (mg + 1) * 128],
                            yg[:, chg * TBS + lo:chg * TBS + hi],
                            start=(chg == 0), stop=(chg == 3)),
                            reads=["woutT", f"yg{chg}"], writes=pskeys(pa, lo, hi))
                op("dve", lambda q, mg=mg, pa=pa: q.tensor_copy(
                    ostage[:, 0:TBS], psum(pa, 0, TBS)),
                    reads=pskeys(pa, 0, TBS), writes=["ostage", "sqM"])
                dma("sp", "ccm", io["ccm_in"][tb * D + mg * 128:tb * D + (mg + 1) * 128, :],
                    ostage[:, 0:TBS], reads=["ostage"])

        if mstop >= 8:
            s.buf["ccmi"] = {"w": ("dma", "ccm", s.dma_cnt.get("ccm", 0)), "r": []}
            coll(lambda q: q.collective_compute(
                "ReduceScatter", ALU.add, replica_groups=GROUPS,
                ins=[io["ccm_in"].opt()], outs=[io["ccm_out"].opt()]),
                "rsm", reads=["ccmi"], writes=["ccmo"])
        s.barrier()
        AB.pop()
        AFR.pop()

        W2K = 2048

        def stats2048(src, srckey_fn, sqt, sqkey, tag, scrA, scrAk, scrB, scrBk):
            for dg in range(8):
                op("act", lambda q, dg=dg: q.activation(
                    sqt[:, 0:W2K], src[:, dg * W2K:(dg + 1) * W2K], AF.Square),
                    reads=srckey_fn(dg), writes=[sqkey])
                for t5 in range(4):
                    lo, hi = t5 * 512, (t5 + 1) * 512
                    op("pe", lambda q, dg=dg, lo=lo, hi=hi: q.matmul(
                        psum("A", lo, hi, rows=1), onescol_bf[:, 0:1],
                        src[:, dg * W2K + lo:dg * W2K + hi],
                        start=(dg == 0), stop=(dg == 7)),
                        reads=srckey_fn(dg) + ["onescol_bf"], writes=pskeys("A", lo, hi))
                    op("pe", lambda q, dg=dg, lo=lo, hi=hi: q.matmul(
                        psum("B", lo, hi, rows=1), onescol_bf[:, 0:1],
                        sqt[:, lo:hi], start=(dg == 0), stop=(dg == 7)),
                        reads=[sqkey, "onescol_bf"], writes=pskeys("B", lo, hi))
            op("dve", lambda q: q.tensor_single_scalar(
                stat[0:1, 0:W2K], psum("A", 0, W2K, rows=1), 1.0 / D, ALU.mult),
                reads=pskeys("A", 0, W2K), writes=["stat_m" + tag])
            op("dve", lambda q: q.tensor_single_scalar(
                scrA[0:1, 0:W2K], psum("B", 0, W2K, rows=1), 1.0 / D, ALU.mult),
                reads=pskeys("B", 0, W2K), writes=[scrAk])
            op("dve", lambda q: q.tensor_tensor(
                scrB[0:1, 0:W2K], stat[0:1, 0:W2K], stat[0:1, 0:W2K], ALU.mult),
                reads=["stat_m" + tag], writes=[scrBk])
            op("dve", lambda q: q.tensor_tensor(
                scrB[0:1, 0:W2K], scrA[0:1, 0:W2K], scrB[0:1, 0:W2K], ALU.subtract),
                reads=[scrAk, scrBk], writes=[scrBk])
            op("act", lambda q: q.activation(
                scrB[0:1, 0:W2K], scrB[0:1, 0:W2K], AF.Ln, bias=EPS),
                reads=[scrBk, "cpool"], writes=[scrBk])
            op("act", lambda q: q.activation(
                stat[32:33, 0:W2K], scrB[0:1, 0:W2K], AF.Exp, scale=-0.5),
                reads=[scrBk], writes=["stat_r" + tag])
            op("dve", lambda q: q.tensor_copy(rowbf[0:1, 0:W2K], stat[0:1, 0:W2K]),
               reads=["stat_m" + tag], writes=["rowbf_m" + tag])
            op("act", lambda q: q.activation(
                rowbf[32:33, 0:W2K], stat[32:33, 0:W2K], AF.Copy),
                reads=["stat_r" + tag], writes=["rowbf_r" + tag])

        def brow(dst, dstkey, rowpart, rowkey):
            for t5 in range(4):
                lo, hi = t5 * 512, (t5 + 1) * 512
                op("pe", lambda q, lo=lo, hi=hi: q.matmul(
                    psum("B", lo, hi), ones_bf[rowpart:rowpart + 1, 0:128],
                    rowbf[rowpart:rowpart + 1, lo:hi], start=True, stop=True),
                    reads=[rowkey, "ones_bf"], writes=pskeys("B", lo, hi))
            op("dve", lambda q: q.tensor_copy(dst[:, 0:W2K], psum("B", 0, W2K)),
               reads=pskeys("B", 0, W2K), writes=[dstkey])

        if mstop >= 8:
            resid = AB.alloc(16 * 1024)
            AB.push()
            AFR.push()
            mo = AB.alloc(16 * 1024)
            bc2 = AB.alloc(2 * W2K)
            sqR = AB.alloc(W2K)
            hst = [AFR.alloc(W2K), AFR.alloc(W2K)]
            for tbl in range(2):
                dma("sp", "moin",
                    mo.rearrange("p (g tb t) -> p g tb t", g=8, tb=2)[:, :, tbl, :],
                    io["ccm_out"][tbl * 1024:(tbl + 1) * 1024, :]
                        .rearrange("(g p) t -> p g t", p=128),
                    reads=["ccmo"], writes=["mo"])
            stats2048(mo, lambda dg: ["mo"], sqR, "sqR", "2", hst[0], "hrs0", hst[1], "hrs1")
            brow(bc2[:, 0:W2K], "m2b", 0, "rowbf_m2")
            brow(bc2[:, W2K:2 * W2K], "r2b", 32, "rowbf_r2")
            for dg in range(8):
                sl = dg % 2
                dma("sp", f"hr{sl}", hst[sl][:, 0:W2K],
                    io["hres"][dg * 128:(dg + 1) * 128, :], writes=[f"hrs{sl}"])
                op("dve", lambda q, dg=dg: q.tensor_tensor(
                    sqR[:, 0:W2K], mo[:, dg * W2K:(dg + 1) * W2K], bc2[:, 0:W2K], ALU.subtract),
                    reads=["mo", "m2b"], writes=["sqR"])
                op("dve", lambda q: q.tensor_tensor(
                    sqR[:, 0:W2K], sqR[:, 0:W2K], bc2[:, W2K:2 * W2K], ALU.mult),
                    reads=["sqR", "r2b"], writes=["sqR"])
                op("dve", lambda q, dg=dg, sl=sl: q.tensor_tensor(
                    resid[:, dg * W2K:(dg + 1) * W2K], hst[sl][:, 0:W2K], sqR[:, 0:W2K], ALU.add),
                    reads=[f"hrs{sl}", "sqR"], writes=[f"resid{dg}"])
            s.barrier()
            AB.pop()
            AFR.pop()

        if mstop >= 9:
            AB.push()
            AFR.push()
            wslot = [AB.alloc(8 * 1024) for _ in range(3)]
            h_sc = AB.alloc(8 * 1024)
            acc = AB.alloc(8 * W2K)
            cwsc = AB.alloc(W2K)
            sqF = cwsc
            bcF = AB.alloc(W2K)
            m4b = wslot[0][:, 0:W2K]
            r4b = wslot[0][:, W2K:2 * W2K]
            lg_em = AFR.alloc(W2K)
            smx = AFR.alloc(64)
            finstage = AFR.alloc(W2K)
            outst = AFR.alloc(128)

            def load_w(k):
                sl = k % 3
                dma("sp", f"ws{sl}", wslot[sl].rearrange("p (g h) -> p g h", g=8),
                    io["wbf"][k, :, :].rearrange("(g p) h -> p g h", p=128),
                    reads=["wbf"], writes=[f"wslot{sl}"])

            stats2048(resid, lambda dg: [f"resid{dg}"], sqF, "cwsc", "3", lg_em, "lg_em", finstage, "finstage")
            op("act", lambda q: q.activation(
                rowbf[0:1, 2048:2048 + W2K], finstage[0:1, 0:W2K], AF.Exp, scale=0.5),
                reads=["finstage"], writes=["rowbf_s3"])
            brow(bcF[:, 0:W2K], "r3b", 32, "rowbf_r3")

            for g in range(8):
                op("pe", lambda q, g=g: q.matmul(
                    psum("A", 0, E, rows=1), onescol_bf[:, 0:1],
                    rw_sb[:, g * E:(g + 1) * E],
                    start=(g == 0), stop=(g == 7)),
                    reads=["rw_sb", "onescol_bf"], writes=pskeys("A", 0, 512))
            op("dve", lambda q: q.tensor_single_scalar(
                wrow[0:1, 1024:1024 + E], psum("A", 0, E, rows=1), -1.0, ALU.mult),
                reads=pskeys("A", 0, 512), writes=["w1r"])
            for tc in range(4):
                lo, hi = tc * 512, (tc + 1) * 512
                for g in range(8):
                    op("pe", lambda q, g=g, tc=tc: q.matmul(
                        psum("A", tc * 512, (tc + 1) * 512, rows=15),
                        rw_sb[:, g * E:(g + 1) * E],
                        resid[:, g * W2K + tc * 512:g * W2K + (tc + 1) * 512],
                        start=(g == 0), stop=False),
                        reads=["rw_sb", f"resid{g}"], writes=pskeys("A", tc * 512, (tc + 1) * 512))
                op("pe", lambda q, tc=tc: q.matmul(
                    psum("A", tc * 512, (tc + 1) * 512, rows=15),
                    wrow[0:1, 1024:1024 + E],
                    rowbf[0:1, tc * 512:(tc + 1) * 512],
                    start=False, stop=False),
                    reads=["w1r", "rowbf_m3"], writes=pskeys("A", tc * 512, (tc + 1) * 512))
                op("pe", lambda q, tc=tc: q.matmul(
                    psum("A", tc * 512, (tc + 1) * 512, rows=15),
                    rbrow[0:1, 0:E],
                    rowbf[0:1, 2048 + tc * 512:2048 + (tc + 1) * 512],
                    start=False, stop=True),
                    reads=["rbrow", "rowbf_s3"], writes=pskeys("A", tc * 512, (tc + 1) * 512))
            op("dve", lambda q: q.tensor_tensor(
                lg_em[0:15, 0:W2K], psum("A", 0, W2K, rows=15), bcF[0:15, 0:W2K], ALU.mult),
                reads=pskeys("A", 0, W2K) + ["r3b"], writes=["lg_em"])

            for tt in range(16):
                op("pe", lambda q, tt=tt: q.transpose(
                    psum("B", 0, 15),
                    lg_em[0:15, tt * 128:(tt + 1) * 128], id128[0:15, 0:15]),
                    reads=["lg_em", "id128"], writes=pskeys("B", 0, 128))
                op("dve", lambda q: q.tensor_copy(smx[:, 0:15], psum("B", 0, 15)),
                   reads=pskeys("B", 0, 128), writes=["lgt"])
                op("dve", lambda q: q.reduce_max(smx[:, 15:16], smx[:, 0:15], AX.X, negate=True),
                   reads=["lgt"], writes=["negm"])
                op("act", lambda q: q.activation(smx[:, 16:31], smx[:, 0:15], AF.Exp,
                                                 bias=smx[:, 15:16]),
                   reads=["lgt"], writes=["ex"], scalar_reads=["negm"])
                op("dve", lambda q: q.reduce_sum(smx[:, 31:32], smx[:, 16:31], AX.X),
                   reads=["ex"], writes=["ssum"])
                op("dve", lambda q: q.reciprocal(smx[:, 32:33], smx[:, 31:32]),
                   reads=["ssum"], writes=["rec"])
                op("dve", lambda q: q.tensor_single_scalar(
                    smx[:, 0:15], smx[:, 16:31], smx[:, 32:33], ALU.mult),
                    reads=["ex"], writes=["probs"], scalar_reads=["rec"])
                op("dve", lambda q: q.reduce_max(smx[:, 15:16], smx[:, 0:15], AX.X),
                   reads=["probs"], writes=["m1"])
                op("dve", lambda q: q.tensor_single_scalar(
                    smx[:, 16:31], smx[:, 0:15], smx[:, 15:16], ALU.is_ge),
                    reads=["probs"], writes=["g1"], scalar_reads=["m1"])
                op("dve", lambda q: q.tensor_scalar(
                    smx[:, 16:31], smx[:, 16:31], -1.0, 1.0, ALU.mult, ALU.add),
                    reads=["g1"], writes=["g1"])
                op("dve", lambda q: q.tensor_tensor(
                    smx[:, 16:31], smx[:, 0:15], smx[:, 16:31], ALU.mult),
                    reads=["probs", "g1"], writes=["p1"])
                op("dve", lambda q: q.reduce_max(smx[:, 15:16], smx[:, 16:31], AX.X),
                   reads=["p1"], writes=["m2"])
                op("dve", lambda q: q.tensor_single_scalar(
                    smx[:, 33:48], smx[:, 16:31], smx[:, 15:16], ALU.is_ge),
                    reads=["p1"], writes=["g2"], scalar_reads=["m2"])
                op("dve", lambda q: q.tensor_scalar(
                    smx[:, 33:48], smx[:, 33:48], -1.0, 1.0, ALU.mult, ALU.add),
                    reads=["g2"], writes=["g2"])
                op("dve", lambda q: q.tensor_tensor(
                    smx[:, 33:48], smx[:, 16:31], smx[:, 33:48], ALU.mult),
                    reads=["p1", "g2"], writes=["p2"])
                op("dve", lambda q: q.reduce_max(smx[:, 15:16], smx[:, 33:48], AX.X),
                   reads=["p2"], writes=["m3"])
                op("dve", lambda q: q.tensor_single_scalar(
                    smx[:, 33:48], smx[:, 0:15], smx[:, 15:16], ALU.is_ge),
                    reads=["probs"], writes=["gk"], scalar_reads=["m3"])
                op("dve", lambda q: q.tensor_tensor(
                    smx[:, 0:15], smx[:, 0:15], smx[:, 33:48], ALU.mult),
                    reads=["probs", "gk"], writes=["cw"])
                op("pe", lambda q, tt=tt: q.transpose(
                    psum("B", 512, 640, rows=15),
                    smx[:, 0:15], id128[:, :]),
                    reads=["cw", "id128"], writes=pskeys("B", 512, 1024))
                op("act", lambda q, tt=tt: q.activation(
                    cwT_sb[0:15, tt * 128:(tt + 1) * 128], psum("B", 512, 640, rows=15),
                    AF.Copy),
                    reads=pskeys("B", 512, 1024), writes=["cwT"])

            for k in range(2):
                load_w(k)
            import os as _os
            _nli = int(_os.environ.get("NLI", "16"))
            for li in (range(_nli) if mstop > 9 else []):
                s1, s2 = (2 * li) % 3, (2 * li + 1) % 3
                slb = li % 2
                if 2 * li + 2 < 2 * _nli:
                    load_w(2 * li + 2)
                if li > 0:
                    dma("sp", f"eb{slb}", ebst[slb][0:1, 0:H],
                        io["ebbf"][:, (li - 1) * H:li * H],
                        reads=["ebbf"], writes=[f"ebr{slb}"])
                    dma("sp", f"eb{slb}", ebst[slb][0:1, H:2 * H],
                        io["ebbf"][:, E * H + (li - 1) * D:E * H + li * D],
                        reads=["ebbf"], writes=[f"ebr{slb}"])
                for t5 in range(2):
                    lo, hi = t5 * 512, (t5 + 1) * 512
                    for g in range(8):
                        op("pe", lambda q, g=g, lo=lo, hi=hi, s1=s1: q.matmul(
                            psum("B", 1024 + lo, 1024 + hi, rows=1), onescol_bf[:, 0:1],
                            wslot[s1][:, g * 1024 + lo:g * 1024 + hi],
                            start=(g == 0), stop=(g == 7)),
                            reads=[f"wslot{s1}", "onescol_bf"],
                            writes=pskeys("B", 1024 + lo, 1024 + hi))
                op("dve", lambda q: q.tensor_single_scalar(
                    wrow[0:1, 0:1024], psum("B", 1024, 2048, rows=1), -1.0, ALU.mult),
                    reads=pskeys("B", 1024, 2048), writes=["w1e"])
                if li > 0:
                    e = li - 1
                    for tc in range(4):
                        op("pe", lambda q, e=e, tc=tc: q.matmul(
                            psum("A", tc * 512, (tc + 1) * 512),
                            sel16[0:15, e * 128:(e + 1) * 128],
                            cwT_sb[0:15, tc * 512:(tc + 1) * 512],
                            start=True, stop=True),
                            reads=["sel16", "cwT"], writes=pskeys("A", tc * 512, (tc + 1) * 512))
                    op("act", lambda q: q.activation(
                        cwrow[0:1, 0:W2K], psum("A", 0, W2K, rows=1), AF.Copy),
                        reads=pskeys("A", 0, W2K), writes=["cwrow"])
                    op("dve", lambda q: q.tensor_tensor(
                        cwsc[:, 0:W2K], psum("A", 0, W2K), bcF[:, 0:W2K], ALU.mult),
                        reads=pskeys("A", 0, W2K) + ["r3b", "cwrow"], writes=["cwsc"])
                for tch in range(2):
                    toff = tch * 1024
                    for hc in range(8):
                        pa = "A" if hc % 2 == 0 else "B"
                        for tc in range(2):
                            lo, hi = tc * 512, (tc + 1) * 512
                            for g in range(8):
                                op("pe", lambda q, g=g, hc=hc, lo=lo, hi=hi, pa=pa,
                                   s1=s1, toff=toff: q.matmul(
                                    psum(pa, lo, hi),
                                    wslot[s1][:, g * 1024 + hc * 128:g * 1024 + (hc + 1) * 128],
                                    resid[:, g * W2K + toff + lo:g * W2K + toff + hi],
                                    start=(g == 0), stop=False),
                                    reads=[f"wslot{s1}", f"resid{g}"], writes=pskeys(pa, lo, hi))
                            op("pe", lambda q, hc=hc, lo=lo, hi=hi, pa=pa, toff=toff: q.matmul(
                                psum(pa, lo, hi), wrow[0:1, hc * 128:(hc + 1) * 128],
                                rowbf[0:1, toff + lo:toff + hi], start=False, stop=False),
                                reads=["w1e", "rowbf_m3"], writes=pskeys(pa, lo, hi))
                            brow_ap = (sb1row[0:1, hc * 128:(hc + 1) * 128] if li == 0
                                       else ebst[slb][0:1, hc * 128:(hc + 1) * 128])
                            brow_key = "sb1row" if li == 0 else f"ebr{slb}"
                            op("pe", lambda q, brow_ap=brow_ap, lo=lo, hi=hi, pa=pa,
                               toff=toff: q.matmul(
                                psum(pa, lo, hi), brow_ap,
                                rowbf[0:1, 2048 + toff + lo:2048 + toff + hi], start=False, stop=True),
                                reads=[brow_key, "rowbf_s3"], writes=pskeys(pa, lo, hi))
                        op("act", lambda q, hc=hc, pa=pa: q.activation(
                            h_sc[:, hc * 1024:(hc + 1) * 1024], psum(pa, 0, 1024), AF.Relu),
                            reads=pskeys(pa, 0, 1024), writes=[f"hsc{hc}"])
                        sc_key = "cwsc" if li > 0 else "r3b"
                        sc_tile = cwsc if li > 0 else bcF
                        op("dve", lambda q, hc=hc, sc_tile=sc_tile, toff=toff: q.tensor_tensor(
                            h_sc[:, hc * 1024:(hc + 1) * 1024], h_sc[:, hc * 1024:(hc + 1) * 1024],
                            sc_tile[:, toff:toff + 1024], ALU.mult),
                            reads=[f"hsc{hc}", sc_key], writes=[f"hsc{hc}"])
                    for dt in range(8):
                        pa = "A" if dt % 2 == 0 else "B"
                        for tc in range(2):
                            lo, hi = tc * 512, (tc + 1) * 512
                            for hc in range(8):
                                op("pe", lambda q, hc=hc, dt=dt, lo=lo, hi=hi, pa=pa,
                                   s2=s2: q.matmul(
                                    psum(pa, lo, hi),
                                    wslot[s2][:, hc * 1024 + dt * 128:hc * 1024 + (dt + 1) * 128],
                                    h_sc[:, hc * 1024 + lo:hc * 1024 + hi],
                                    start=(hc == 0), stop=False),
                                    reads=[f"wslot{s2}", f"hsc{hc}"], writes=pskeys(pa, lo, hi))
                            if li == 0:
                                op("pe", lambda q, dt=dt, lo=lo, hi=hi, pa=pa: q.matmul(
                                    psum(pa, lo, hi),
                                    sb2row[0:1, dt * 128:(dt + 1) * 128],
                                    ones_bf[0:1, 0:512], start=False, stop=True),
                                    reads=["sb2row", "ones_bf"], writes=pskeys(pa, lo, hi))
                            else:
                                op("pe", lambda q, dt=dt, lo=lo, hi=hi, pa=pa, slb=slb,
                                   toff=toff: q.matmul(
                                    psum(pa, lo, hi),
                                    ebst[slb][0:1, H + dt * 128:H + (dt + 1) * 128],
                                    cwrow[0:1, toff + lo:toff + hi], start=False, stop=True),
                                    reads=[f"ebr{slb}", "cwrow"], writes=pskeys(pa, lo, hi))
                        if li == 0:
                            op("dve", lambda q, dt=dt, pa=pa, toff=toff: q.tensor_copy(
                                acc[:, dt * W2K + toff:dt * W2K + toff + 1024],
                                psum(pa, 0, 1024)),
                                reads=pskeys(pa, 0, 1024), writes=[f"acc{dt}"])
                        else:
                            op("dve", lambda q, dt=dt, pa=pa, toff=toff: q.tensor_tensor(
                                acc[:, dt * W2K + toff:dt * W2K + toff + 1024],
                                acc[:, dt * W2K + toff:dt * W2K + toff + 1024],
                                psum(pa, 0, 1024), ALU.add),
                                reads=[f"acc{dt}"] + pskeys(pa, 0, 1024), writes=[f"acc{dt}"])
                if 2 * li + 3 < 2 * _nli and tch == 1 and dt == 7:
                    load_w(2 * li + 3)


            do_final = mstop > 10
            if do_final:
                stats2048(acc, lambda dg: [f"acc{dg}"], sqF, "cwsc", "4", lg_em, "lg_em", finstage, "finstage")
                brow(m4b, "wslot0", 0, "rowbf_m4")
            
                brow(r4b, "wslot0", 32, "rowbf_r4")
                for dg in range(8):
                    op("dve", lambda q, dg=dg: q.tensor_tensor(
                        sqF[:, 0:W2K], acc[:, dg * W2K:(dg + 1) * W2K],
                        m4b, ALU.subtract),
                        reads=[f"acc{dg}", "wslot0"], writes=["sqFn"])
                    op("dve", lambda q: q.tensor_tensor(
                        sqF[:, 0:W2K], sqF[:, 0:W2K], r4b, ALU.mult),
                        reads=["sqFn", "wslot0"], writes=["sqFn"])
                    op("dve", lambda q, dg=dg: q.tensor_tensor(
                        finstage[:, 0:W2K], resid[:, dg * W2K:(dg + 1) * W2K],
                        sqF[:, 0:W2K], ALU.add),
                        reads=[f"resid{dg}", "sqFn"], writes=["finstage"])
                    for tt in range(16):
                        po = (tt % 4) * 512
                        op("pe", lambda q, tt=tt, po=po: q.transpose(
                            psum("B", po, po + 128), finstage[:, tt * 128:(tt + 1) * 128],
                            id128[:, :]),
                            reads=["finstage", "id128"], writes=pskeys("B", po, po + 128))
                        op("dve", lambda q, po=po: q.tensor_copy(
                            outst[:, 0:128], psum("B", po, po + 128)),
                            reads=pskeys("B", po, po + 128), writes=["outst"])
                        dma("sp", "out", io["out"][tt * 128:(tt + 1) * 128, dg * 128:(dg + 1) * 128],
                            outst[:, 0:128], reads=["outst"])

        if dbg_names:
            s.barrier()
            named = {
                "hbf": hbf, "x_s": x_s, "sz": sz, "delta": delta, "dx": dx,
                "yg": yg, "xdbl": xdbl, "xdbl_c": xdbl_c,
                "carry0": carry[0][:, :], "carry1": carry[1][:, :],
                "yacc": yacc,
                "s_t": s_t[0].bitcast(BF16), "b_t": b_t, "a_t2": a_t[0].bitcast(BF16),
            }
            if mstop >= 8:
                named["resid"] = resid
                named["mo"] = mo
            if mstop >= 9:
                named["cwT"] = cwT_sb[0:15, :]
                named["lg_em"] = lg_em[0:15, 0:2048]
                named["cwsc"] = cwsc[:, 0:2048]
                named["hsc"] = h_sc
                named["ws0"] = wslot[0]
                named["ws2"] = wslot[2]
            if mstop >= 9:
                named["acc"] = acc
            for i, nm in enumerate(dbg_names):
                src = named[nm]
                p, cols = src.shape[0], src.shape[1]
                dma("pool", "dbg", dbg[i][0:p, 0:cols], src[0:p, 0:cols])
        s.barrier()
    with nc.Block() as block:
        s.replay(block)
    es.close()
    return nc, io




def kernel(**inputs):
    from concourse.bass_utils import run_bass_kernel_spmd
    maps = host_prep(inputs)
    nc, io = build_program("full")
    res = run_bass_kernel_spmd(nc, maps, core_ids=list(range(NCORE)))
    outs = []
    for bb in range(B):
        rows = [np.asarray(res.results[4 * bb + q]["out"]) for q in range(4)]
        outs.append(np.concatenate(rows, axis=0))
    return np.stack(outs).astype(np.float32)


# version 07
# speedup vs baseline: 1.2124x; 1.0001x over previous
import sys
try:
    import concourse.bass
except ImportError:
    sys.path.insert(0, "/opt/trn_rl_repo")

"""Megakernel builder for mk_sandwich_d1 (Mamba-S6 + dense-MoE sandwich).

Sharding (8 cores = one TRN2 chip):
  Mamba : core c -> batch b=c//4, d_inner quarter q=c%4 (512 channels),
          full sequence (scan entirely on-core). x_dbl partials AllReduced
          per 1024-token block across the 4 cores of the batch; out_proj
          partials ReduceScattered to the FFN token owners.
  FFN   : core c -> 2048 tokens (quarter q of batch b); dense experts with
          top-3 routed weights; the pre-FFN LN is folded into the matmuls
          (rank-1 mean fixup; rstd applied after relu; biases pre-divided
          by rstd via a std-row bias matmul).
"""
import contextlib
import numpy as np
import ml_dtypes

import concourse.bass as bass
import concourse.mybir as mybir

F32 = mybir.dt.float32
BF16 = mybir.dt.bfloat16
AF = mybir.ActivationFunctionType
ALU = mybir.AluOpType
AX = mybir.AxisListType
bf = ml_dtypes.bfloat16

B, L, D = 2, 8192, 1024
DI, DS, DC, DTR = 2048, 16, 4, 64
E, H = 15, 1024
NCORE = 8
CH = DI // 4
TOK = L // 4
TBS = 1024
NTB = L // TBS
EPS = 1e-5
GROUPS = [[0, 1, 2, 3], [4, 5, 6, 7]]
ENGS = ("pe", "act", "dve", "pool", "sp")


class Sync:

    def __init__(self, nc, es):
        self.nc = nc
        self.eng = {}
        self.insts = {k: [] for k in ENGS}
        self.incs = {k: [] for k in ENGS}
        self.sem = {k: es.enter_context(nc.semaphore(f"sem_{k}")) for k in ENGS}
        self.clock = {a: {b: -1 for b in ENGS} for a in ENGS}
        self.dma_sems = {}
        self.dma_cnt = {}
        self.dma_clock = {a: {} for a in ENGS}
        self.cc_clock = {}
        self._cc = {}
        self.buf = {}
        self.es = es

    def set_queues(self, **eng_objs):
        self.eng = eng_objs

    def _dma_sem(self, tag):
        if tag not in self.dma_sems:
            self.dma_sems[tag] = self.es.enter_context(
                self.nc.semaphore(f"dma_{tag}"))
            self.dma_cnt[tag] = 0
        return self.dma_sems[tag]

    def _ensure_inc(self, En, i):
        incs = self.incs[En]
        for pos, k in enumerate(incs):
            if k >= i:
                return self.sem[En], pos + 1
        k = len(self.insts[En]) - 1
        while k >= 0 and (self.insts[En][k]["fn"] is None
                          or self.insts[En][k].get("dma")):
            k -= 1
        assert k >= i >= 0, f"dep on future inst {En}:{i} (have {k})"
        self.insts[En][k]["incs"].append((self.sem[En], 1))
        incs.append(k)
        return self.sem[En], len(incs)

    def _dep(self, eng, w, waits, dist_guard=2):
        kind, En, i = w
        if kind == "cc":
            ck = self.cc_clock.setdefault(eng, set())
            if En in ck:
                return
            waits.append((self._cc[En], 1))
            ck.add(En)
            return
        if kind == "dma":
            cnt = self.dma_cnt[En]
            if self.dma_clock[eng].get(En, 0) >= cnt:
                return
            waits.append((self._dma_sem(En), 16 * cnt))
            self.dma_clock[eng][En] = cnt
            return
        if En == eng:
            idx = len(self.insts[eng])
            if eng == "pe" or idx - i > dist_guard:
                return
        elif self.clock[eng][En] >= i:
            return
        sem, val = self._ensure_inc(En, i)
        waits.append((sem, val))
        if En != eng:
            self.clock[eng][En] = self.incs[En][val - 1]

    def _gather(self, eng, reads, writes, scalar_reads):
        waits = []
        for key in reads:
            ent = self.buf.get(key)
            if ent and ent["w"]:
                self._dep(eng, ent["w"], waits)
        for key in scalar_reads:
            ent = self.buf.get(key)
            if ent and ent["w"]:
                self._dep(eng, ent["w"], waits, dist_guard=4)
        for key in writes:
            ent = self.buf.get(key)
            if ent:
                if ent["w"]:
                    self._dep(eng, ent["w"], waits)
                for r in ent["r"]:
                    self._dep(eng, r, waits)
        ded = {}
        for sem, val in waits:
            k = id(sem)
            if k not in ded or ded[k][1] < val:
                ded[k] = (sem, val)
        return list(ded.values())

    def _record(self, who, reads, writes, scalar_reads=()):
        for key in writes:
            self.buf[key] = {"w": who, "r": []}
        for key in list(reads) + list(scalar_reads):
            ent = self.buf.setdefault(key, {"w": None, "r": []})
            ent["r"].append(who)

    def op(self, eng, fn, reads=(), writes=(), scalar_reads=()):
        waits = self._gather(eng, reads, writes, scalar_reads)
        rec = {"waits": waits, "fn": fn, "incs": []}
        idx = len(self.insts[eng])
        self.insts[eng].append(rec)
        self._record(("eng", eng, idx), reads, writes, scalar_reads)
        return rec

    def dma(self, queue, tag, out, in_, reads=(), writes=()):
        waits = self._gather(queue, reads, writes, ())
        sem = self._dma_sem(tag)
        rec = {"waits": waits,
               "fn": (lambda q, out=out, in_=in_: q.dma_start(out=out, in_=in_)),
               "incs": [(sem, 16)], "dma": True}
        self.insts[queue].append(rec)
        self.dma_cnt[tag] += 1
        self._record(("dma", tag, self.dma_cnt[tag]), reads, writes)
        return rec

    def collective(self, fn, tag, reads=(), writes=()):
        waits = self._gather("pool", reads, writes, ())
        sem = self.es.enter_context(self.nc.semaphore(f"cc_{tag}"))
        self._cc[tag] = sem
        rec = {"waits": waits, "fn": fn, "incs": [(sem, None)], "dma": True}
        self.insts["pool"].append(rec)
        self._record(("cc", tag, 1), reads, writes)
        return rec

    def barrier(self):
        for En in ENGS:
            eng_insts = [i for i, r in enumerate(self.insts[En])
                         if r["fn"] is not None and not r.get("dma")]
            if eng_insts:
                self._ensure_inc(En, eng_insts[-1])
        for eng in ENGS:
            waits = []
            for En in ENGS:
                if En == eng:
                    continue
                if self.incs[En]:
                    waits.append((self.sem[En], len(self.incs[En])))
                    self.clock[eng][En] = self.incs[En][-1]
            for tag, cnt in list(self.dma_cnt.items()):
                if cnt > 0 and self.dma_clock[eng].get(tag, 0) < cnt:
                    waits.append((self.dma_sems[tag], 16 * cnt))
                    self.dma_clock[eng][tag] = cnt
            for tag in self._cc:
                ck = self.cc_clock.setdefault(eng, set())
                if tag not in ck:
                    waits.append((self._cc[tag], 1))
                    ck.add(tag)
            self.insts[eng].append({"waits": waits, "fn": None, "incs": []})

    def replay(self, block):
        def runner(eng):
            def body(q):
                for rec in self.insts[eng]:
                    for sem, val in rec["waits"]:
                        q.wait_ge(sem, val)
                    if rec["fn"] is None:
                        continue
                    inst = rec["fn"](q)
                    for sem, amt in rec["incs"]:
                        if amt is None:
                            inst.then_inc(sem)
                        else:
                            inst.then_inc(sem, amt)
            return body
        block.tensor(runner("pe"))
        block.scalar(runner("act"))
        block.vector(runner("dve"))
        block.gpsimd(runner("pool"))
        block.sync(runner("sp"))


class Arena:
    def __init__(self, nc, es, name, dt, cols):
        self.t = es.enter_context(nc.sbuf_tensor(name, [128, cols], dt))
        self.off = 0
        self.cols = cols
        self.marks = []

    def alloc(self, cols):
        off = self.off
        self.off += cols
        assert self.off <= self.cols, f"arena overflow {self.off}>{self.cols}"
        return self.t[0:128, off:off + cols]

    def push(self):
        self.marks.append(self.off)

    def pop(self):
        self.off = self.marks.pop()


def host_prep(inputs):
    h = np.asarray(inputs["L0_attn_hidden_states"])
    W_in = np.asarray(inputs["L0_attn_W_in"])
    C = np.ascontiguousarray
    sel16 = np.zeros((16, 16 * 128), dtype=bf)
    for n in range(16):
        sel16[n, n * 128:(n + 1) * 128] = 1.0
    id128 = np.eye(128, dtype=np.float32)
    ones_row = np.ones((1, 512), dtype=bf)
    hT = [C(h[b].T) for b in range(B)]
    maps = []
    for c in range(NCORE):
        b, q = c // 4, c % 4
        chs = np.arange(CH * q, CH * (q + 1))
        Wsl = np.concatenate([W_in[2 * chs], W_in[2 * chs + 1]], axis=0)
        m = {
            "hT": hT[b],
            "hres": C(h[b, TOK * q:TOK * (q + 1), :].T),
            "w_inT": C(Wsl.T),
            "w_outT": C(np.asarray(inputs["L0_attn_W_out"])[:, chs].T),
            "xpwT": C(np.asarray(inputs["L0_attn_x_proj_w"])[:, chs].T),
            "dtwT": C(np.asarray(inputs["L0_attn_dt_w"])[chs, :].T),
            "alog": C(np.asarray(inputs["L0_attn_A_log"])[chs]),
            "convw": C(np.asarray(inputs["L0_attn_conv_w"])[chs]),
            "convb": C(np.asarray(inputs["L0_attn_conv_b"])[chs]),
            "dtb": C(np.asarray(inputs["L0_attn_dt_b"])[chs]),
            "dvec": C(np.asarray(inputs["L0_attn_D"])[chs]),
            "rw": np.asarray(inputs["L0_ffn_router_W"]),
            "rb": np.asarray(inputs["L0_ffn_router_b"]).reshape(1, E),
            "sw1": np.asarray(inputs["L0_ffn_sW1"]),
            "sw2": np.asarray(inputs["L0_ffn_sW2"]),
            "ew1": np.asarray(inputs["L0_ffn_eW1"]),
            "ew2": np.asarray(inputs["L0_ffn_eW2"]),
            "sb1r": np.asarray(inputs["L0_ffn_sb1"]).reshape(1, H),
            "sb2r": np.asarray(inputs["L0_ffn_sb2"]).reshape(1, D),
            "eb1r": np.asarray(inputs["L0_ffn_eb1"]).reshape(1, E * H),
            "eb2r": np.asarray(inputs["L0_ffn_eb2"]).reshape(1, E * D),
            "sel16": sel16, "id128": id128, "ones_row": ones_row,
        }
        maps.append(m)
    return maps


def declare_io(nc):
    d = {}
    def di(name, shape, dt=F32):
        d[name] = nc.dram_tensor(name, list(shape), dt, kind="ExternalInput").ap()
    di("hT", (D, L)); di("hres", (D, TOK))
    di("w_inT", (D, 2 * CH)); di("w_outT", (CH, D))
    di("xpwT", (CH, 96)); di("dtwT", (DTR, CH))
    di("alog", (CH, DS)); di("convw", (CH, DC))
    di("convb", (CH,)); di("dtb", (CH,)); di("dvec", (CH,))
    di("rw", (D, E)); di("rb", (1, E))
    di("sw1", (D, H)); di("sw2", (H, D))
    di("ew1", (E, D, H)); di("ew2", (E, H, D))
    di("sb1r", (1, H)); di("sb2r", (1, D))
    di("eb1r", (1, E * H)); di("eb2r", (1, E * D))
    di("sel16", (16, 16 * 128), BF16)
    di("id128", (128, 128))
    di("ones_row", (1, 512), BF16)
    d["out"] = nc.dram_tensor("out", [TOK, D], F32, kind="ExternalOutput").ap()
    for tb in range(NTB):
        d[f"ccx_in{tb}"] = nc.dram_tensor(f"ccx_in{tb}", [96, TBS], BF16, kind="Internal").ap()
        d[f"ccx_out{tb}"] = nc.dram_tensor(f"ccx_out{tb}", [96, TBS], BF16, kind="Internal").ap()
    d["ccm_in"] = nc.dram_tensor("ccm_in", [NTB * D, TBS], BF16, kind="Internal").ap()
    d["wbf"] = nc.dram_tensor("wbf", [32, D, H], BF16, kind="Internal").ap()
    d["ebbf"] = nc.dram_tensor("ebbf", [1, 2 * E * H], BF16, kind="Internal").ap()
    d["ccm_out"] = nc.dram_tensor("ccm_out", [2 * D, TBS], BF16, kind="Internal").ap()
    return d


def build_program(upto="full", dbg_names=()):
    nc = bass.Bass("TRN2", target_bir_lowering=False)
    io = declare_io(nc)
    es = contextlib.ExitStack()
    s = Sync(nc, es)

    dbg = []
    for i in range(len(dbg_names)):
        dbg.append(nc.dram_tensor(f"dbg{i}", [128, 16 * 1024], F32, kind="ExternalOutput").ap())

    sb_ = lambda n, shape, dt: es.enter_context(nc.sbuf_tensor(n, shape, dt))

    xpwT = sb_("sb_xpwT", [128, 4 * 96], BF16)
    dtwT = sb_("sb_dtwT", [64, 4 * 128], BF16)
    A_sb = sb_("A_sb", [128, 4 * 16], F32)
    convw = sb_("sb_convw", [128, 4 * 4], F32)
    convb = sb_("sb_convb", [128, 4], F32)
    dtb = sb_("sb_dtb", [128, 4], F32)
    dvec = sb_("sb_dvec", [128, 4], F32)
    ndtb = sb_("sb_ndtb", [128, 4], F32)
    rw_sb = sb_("rw_sb", [128, 8 * E], BF16)
    sel16 = sb_("sb_sel16", [16, 16 * 128], BF16)
    id128 = sb_("sb_id128", [128, 128], F32)
    id128b = sb_("sb_id128b", [128, 128], BF16)
    ones_bf = sb_("ones_bf", [128, 512], BF16)
    onescol_bf = sb_("onescol_bf", [128, 1], BF16)
    sb1row = sb_("sb1row", [1, H], BF16)
    sb2row = sb_("sb2row", [1, D], BF16)
    rbrow = sb_("rbrow", [1, E], BF16)
    ebst = [sb_("ebst0", [1, 2 * H], BF16), sb_("ebst1", [1, 2 * H], BF16)]
    negw1 = sb_("negw1", [1, 2 * CH], BF16)
    xdbl = sb_("xdbl", [80, TBS], BF16)
    xdbl_c = sb_("xdbl_c", [16, TBS], BF16)
    xdbl_b = sb_("xdbl_b", [16, TBS], BF16)
    carry = [sb_("carry0", [128, 64], F32), sb_("carry1", [128, 64], F32)]
    stat = sb_("stat", [128, 2048], F32)
    rowbf = sb_("rowbf", [128, 4096], BF16)
    wrow = sb_("wrow", [1, 1040], BF16)
    cwrow = sb_("cwrow", [1, 2048], BF16)
    cwT_sb = sb_("cwT_sb", [15, 2048], BF16)
    cpool = sb_("cpool", [1, 4], F32)
    nc.const_aps.register(F32, EPS, cpool[0:1, 0:1])

    AB = Arena(nc, es, "AB", BF16, 69696)
    AFR = Arena(nc, es, "AFR", F32, 4320)
    psA_t = es.enter_context(nc.psum_tensor("psA", [128, 2048], F32))
    psB_t = es.enter_context(nc.psum_tensor("psB", [128, 2048], F32))

    if True:
        op, dma, coll = s.op, s.dma, s.collective

        def psum(arena, lo, hi, rows=128, base=0):
            t = psA_t if arena == "A" else psB_t
            return t[base:base + rows, lo:hi]

        def pskeys(arena, lo, hi):
            return [f"ps{arena}{i}" for i in range(lo // 512, (hi + 511) // 512)]

        dma("pool", "prep2", xpwT.ap().rearrange("p (g e) -> p g e", g=4),
            io["xpwT"].rearrange("(g p) e -> p g e", p=128), writes=["xpwT"])
        dma("pool", "prep2", dtwT.ap().rearrange("r (g c) -> r g c", g=4),
            io["dtwT"].rearrange("r (g c) -> r g c", g=4), writes=["dtwT"])
        dma("pool", "prep2", rw_sb.ap().rearrange("p (g e) -> p g e", g=8),
            io["rw"].rearrange("(g p) e -> p g e", p=128), writes=["rw_sb"])
        dma("sp", "prep", sel16[0:16, :], io["sel16"][:, :], writes=["sel16"])
        dma("sp", "prep", id128[:, :], io["id128"][:, :], writes=["id128"])
        op("act", lambda q: q.activation(id128b[:, :], id128[:, :], AF.Copy),
           reads=["id128"], writes=["id128b"])
        op("dve", lambda q: q.memset(ones_bf[:, :], 1.0), writes=["ones_bf"])
        for g in range(4):
            dma("sp", "prep", convw[:, g * 4:(g + 1) * 4],
                io["convw"][g * 128:(g + 1) * 128, :], writes=["convw"])
            dma("sp", "prep", convb[:, g:g + 1],
                io["convb"][g * 128:(g + 1) * 128].unsqueeze(1), writes=["convb"])
            dma("sp", "prep", dtb[:, g:g + 1],
                io["dtb"][g * 128:(g + 1) * 128].unsqueeze(1), writes=["dtb"])
            dma("sp", "prep", dvec[:, g:g + 1],
                io["dvec"][g * 128:(g + 1) * 128].unsqueeze(1), writes=["dvec"])
        dma("pool", "prep2", sb1row[0:1, :], io["sb1r"][:, :], writes=["sb1row"])
        dma("pool", "prep2", sb2row[0:1, :], io["sb2r"][:, :], writes=["sb2row"])
        dma("pool", "prep2", rbrow[0:1, :], io["rb"][:, :], writes=["rbrow"])
        op("dve", lambda q: q.memset(onescol_bf[:, :], 1.0), writes=["onescol_bf"])
        for k in range(32):
            li_k, isw1 = k // 2, (k % 2 == 0)
            if li_k == 0:
                srcw = io["sw1"] if isw1 else io["sw2"]
            else:
                srcw = (io["ew1"] if isw1 else io["ew2"])[li_k - 1, :, :]
            dma("pool", "wcast", io["wbf"][k, :, :], srcw, writes=["wbf"])
        dma("pool", "wcast", io["ebbf"][:, 0:E * H], io["eb1r"][:, :], writes=["ebbf"])
        dma("pool", "wcast", io["ebbf"][:, E * H:2 * E * H], io["eb2r"][:, :], writes=["ebbf"])
        op("dve", lambda q: q.memset(cpool[0:1, 0:1], EPS), writes=["cpool"])
        op("dve", lambda q: q.memset(carry[0][:, :], 0.0), writes=["carry0"])
        op("dve", lambda q: q.memset(carry[1][:, :], 0.0), writes=["carry1"])
        for _ln in range(2):
            for _pp in range(2):
                op("dve", lambda q, _ln=_ln, _pp=_pp: q.memset(a_t[_ln][_pp][:, 0:1], 0.0),
                   writes=[f"a_t{_ln}_{_pp}"])
        op("dve", lambda q: q.tensor_single_scalar(ndtb[:, :], dtb[:, :], -1.0, ALU.mult),
           reads=["dtb"], writes=["ndtb"])

        AB.push()
        AFR.push()
        winT = AB.alloc(8 * 1024)
        woutT = AB.alloc(4 * 1024)
        hbf = AB.alloc(8 * TBS)
        Brep = [AB.alloc(TBS) for _ in range(2)]
        Crep = [AB.alloc(TBS) for _ in range(2)]
        b_t = AB.alloc(TBS + 2)
        s_t = [AB.alloc(2 * TBS + 4).bitcast(F32) for _ in range(2)]
        w_t = AB.alloc(TBS)
        sqM = AB.alloc(TBS)
        tmpM = Brep[0]
        ostage = sqM
        x_raw = [AB.alloc(TBS + 4) for _ in range(4)]
        x_s = AB.alloc(4 * TBS)
        sz = AB.alloc(4 * TBS)
        delta = AB.alloc(4 * TBS)
        dx = AB.alloc(4 * TBS)
        yg = AB.alloc(4 * TBS)
        w_t2 = AB.alloc(TBS)
        a_t = [[AB.alloc(TBS + 2) for _ in range(2)] for _ in range(2)]
        yacc = AB.alloc(4 * TBS)
        bcast_bf = w_t
        stage = AFR.alloc(TBS)
        stage2 = AFR.alloc(TBS)
        ytmp = AFR.alloc(TBS)

        dma("pool", "winT", winT.rearrange("p (g e) -> p g e", g=8),
            io["w_inT"].rearrange("(g p) e -> p g e", p=128), writes=["winT"])
        dma("pool", "woutT", woutT.rearrange("p (g m) -> p g m", g=4),
            io["w_outT"].rearrange("(g p) m -> p g m", p=128), writes=["woutT"])
        dma("sp", "alog", stage[:, 0:64].rearrange("p (g n) -> p g n", g=4),
            io["alog"].rearrange("(g p) n -> p g n", p=128), writes=["stage"])
        op("act", lambda q: q.activation(A_sb[:, :], stage[:, 0:64], AF.Exp),
           reads=["stage"], writes=["A_sb"])
        op("dve", lambda q: q.tensor_single_scalar(A_sb[:, :], A_sb[:, :], -1.0, ALU.mult),
           reads=["A_sb"], writes=["A_sb"])

        for t5 in range(2):
            lo, hi = t5 * 512, (t5 + 1) * 512
            for g in range(8):
                op("pe", lambda q, g=g, lo=lo, hi=hi: q.matmul(
                    psum("A", lo, hi, rows=1), onescol_bf[:, 0:1],
                    winT[:, g * 1024 + lo:g * 1024 + hi],
                    start=(g == 0), stop=(g == 7)),
                    reads=["winT", "onescol_bf"], writes=pskeys("A", lo, hi))
        op("dve", lambda q: q.tensor_single_scalar(
            negw1[0:1, :], psum("A", 0, 1024, rows=1), -1.0, ALU.mult),
            reads=pskeys("A", 0, 1024), writes=["negw1"])

        mstop = {"inproj": 0, "conv": 1, "xproj": 2, "delta": 3, "scan": 5,
                 "outproj": 6, "mamba": 7, "resid": 8, "router": 9,
                 "experts": 10}.get(upto, 99)
        import os
        ntb_run = NTB if mstop >= 7 else 1
        if os.environ.get("NTB_OVERRIDE"):
            ntb_run = int(os.environ["NTB_OVERRIDE"])

        gidx = 0
        for tb in range(ntb_run):
            t0 = tb * TBS
            dma("pool", "hbf", hbf.rearrange("p (g t) -> p g t", g=8),
                io["hT"][:, t0:t0 + TBS].rearrange("(g p) t -> p g t", p=128),
                writes=["hbf"])
            for dg in range(8):
                op("act", lambda q, dg=dg: q.activation(
                    sqM[:, 0:TBS], hbf[:, dg * TBS:(dg + 1) * TBS], AF.Square),
                    reads=["hbf"], writes=["sqM", "ostage"])
                for t5 in range(2):
                    lo, hi = t5 * 512, (t5 + 1) * 512
                    op("pe", lambda q, dg=dg, lo=lo, hi=hi: q.matmul(
                        psum("A", lo, hi, rows=1), onescol_bf[:, 0:1],
                        hbf[:, dg * TBS + lo:dg * TBS + hi],
                        start=(dg == 0), stop=(dg == 7)),
                        reads=["hbf", "onescol_bf"], writes=pskeys("A", lo, hi))
                    op("pe", lambda q, dg=dg, lo=lo, hi=hi: q.matmul(
                        psum("B", lo, hi, rows=1), onescol_bf[:, 0:1],
                        sqM[:, lo:hi], start=(dg == 0), stop=(dg == 7)),
                        reads=["sqM", "onescol_bf"], writes=pskeys("B", lo, hi))
            op("dve", lambda q: q.tensor_single_scalar(
                stat[0:1, 0:TBS], psum("A", 0, TBS, rows=1), 1.0 / D, ALU.mult),
                reads=pskeys("A", 0, TBS), writes=["stat_m1"])
            op("dve", lambda q: q.tensor_single_scalar(
                stage[0:1, 0:TBS], psum("B", 0, TBS, rows=1), 1.0 / D, ALU.mult),
                reads=pskeys("B", 0, TBS), writes=["stage"])
            op("dve", lambda q: q.tensor_tensor(
                stage2[0:1, 0:TBS], stat[0:1, 0:TBS], stat[0:1, 0:TBS], ALU.mult),
                reads=["stat_m1"], writes=["stage2"])
            op("dve", lambda q: q.tensor_tensor(
                stage2[0:1, 0:TBS], stage[0:1, 0:TBS], stage2[0:1, 0:TBS], ALU.subtract),
                reads=["stage", "stage2"], writes=["stage2"])
            op("act", lambda q: q.activation(
                stage2[0:1, 0:TBS], stage2[0:1, 0:TBS], AF.Ln, bias=EPS),
                reads=["stage2", "cpool"], writes=["stage2"])
            op("act", lambda q: q.activation(
                stat[32:33, 0:TBS], stage2[0:1, 0:TBS], AF.Exp, scale=-0.5),
                reads=["stage2"], writes=["stat_r1"])
            op("dve", lambda q: q.tensor_copy(rowbf[0:1, 0:TBS], stat[0:1, 0:TBS]),
               reads=["stat_m1"], writes=["rowbf_m1"])
            op("act", lambda q: q.activation(
                rowbf[32:33, 0:TBS], stat[32:33, 0:TBS], AF.Copy),
                reads=["stat_r1"], writes=["rowbf_r1"])
            for t5 in range(2):
                lo, hi = t5 * 512, (t5 + 1) * 512
                op("pe", lambda q, lo=lo, hi=hi: q.matmul(
                    psum("B", lo, hi), ones_bf[32:33, 0:128],
                    rowbf[32:33, lo:hi], start=True, stop=True),
                    reads=["rowbf_r1", "ones_bf"], writes=pskeys("B", lo, hi))
            op("dve", lambda q: q.tensor_copy(bcast_bf[:, 0:TBS], psum("B", 0, TBS)),
               reads=pskeys("B", 0, TBS), writes=["rstd1b", "w_t"])

            for ec in range(8):
                pa = "A" if ec % 2 == 0 else "B"
                for t5 in range(2):
                    lo, hi = t5 * 512, (t5 + 1) * 512
                    for dg in range(8):
                        op("pe", lambda q, ec=ec, dg=dg, lo=lo, hi=hi, pa=pa: q.matmul(
                            psum(pa, lo, hi),
                            winT[:, dg * 1024 + ec * 128:dg * 1024 + (ec + 1) * 128],
                            hbf[:, dg * TBS + lo:dg * TBS + hi],
                            start=(dg == 0), stop=False),
                            reads=["winT", "hbf"], writes=pskeys(pa, lo, hi))
                    op("pe", lambda q, ec=ec, lo=lo, hi=hi, pa=pa: q.matmul(
                        psum(pa, lo, hi),
                        negw1[0:1, ec * 128:(ec + 1) * 128],
                        rowbf[0:1, lo:hi],
                        start=False, stop=True),
                        reads=["negw1", "rowbf_m1"], writes=pskeys(pa, lo, hi))
                if ec < 4:
                    chg = ec
                    if tb == 0:
                        op("dve", lambda q, chg=chg: q.memset(x_raw[chg][:, 0:3], 0.0),
                           writes=[f"x_raw{chg}"])
                    else:
                        op("dve", lambda q, chg=chg: q.tensor_copy(
                            x_raw[chg][:, 0:3], x_raw[chg][:, TBS:TBS + 3]),
                            reads=[f"x_raw{chg}"], writes=[f"x_raw{chg}"])
                    op("dve", lambda q, chg=chg, pa=pa: q.tensor_tensor(
                        x_raw[chg][:, 3:3 + TBS], psum(pa, 0, TBS),
                        bcast_bf[:, 0:TBS], ALU.mult),
                        reads=pskeys(pa, 0, TBS) + ["rstd1b"], writes=[f"x_raw{chg}"])
                else:
                    chg = ec - 4
                    op("dve", lambda q, pa=pa: q.tensor_tensor(
                        stage[:, 0:TBS], psum(pa, 0, TBS), bcast_bf[:, 0:TBS], ALU.mult),
                        reads=pskeys(pa, 0, TBS) + ["rstd1b"], writes=["stage"])
                    op("act", lambda q, chg=chg: q.activation(
                        sz[:, chg * TBS:(chg + 1) * TBS], stage[:, 0:TBS], AF.Silu),
                        reads=["stage"], writes=[f"sz{chg}"])
            if mstop <= 0:
                break

            for chg in range(4):
                op("dve", lambda q, chg=chg: q.tensor_single_scalar(
                    stage2[:, 0:TBS], x_raw[chg][:, 0:TBS],
                    convw[:, chg * 4:chg * 4 + 1], ALU.mult),
                    reads=[f"x_raw{chg}"], writes=["stage2"], scalar_reads=["convw"])
                for k in range(1, 4):
                    op("dve", lambda q, chg=chg, k=k: q.scalar_tensor_tensor(
                        stage2[:, 0:TBS], x_raw[chg][:, k:k + TBS],
                        convw[:, chg * 4 + k:chg * 4 + k + 1], stage2[:, 0:TBS],
                        ALU.mult, ALU.add),
                        reads=[f"x_raw{chg}", "stage2"], writes=["stage2"],
                        scalar_reads=["convw"])
                op("act", lambda q, chg=chg: q.activation(
                    x_s[:, chg * TBS:(chg + 1) * TBS], stage2[:, 0:TBS], AF.Silu,
                    bias=convb[:, chg:chg + 1]),
                    reads=["stage2", "convb"], writes=[f"x_s{chg}"])
            if mstop <= 1:
                break

            for t5 in range(2):
                lo, hi = t5 * 512, (t5 + 1) * 512
                for chg in range(4):
                    op("pe", lambda q, chg=chg, lo=lo, hi=hi: q.matmul(
                        psum("A", lo, hi, rows=80),
                        xpwT[:, chg * 96:chg * 96 + 80],
                        x_s[:, chg * TBS + lo:chg * TBS + hi],
                        start=(chg == 0), stop=(chg == 3)),
                        reads=["xpwT", f"x_s{chg}"], writes=pskeys("A", lo, hi))
                    op("pe", lambda q, chg=chg, lo=lo, hi=hi: q.matmul(
                        psum("B", lo, hi, rows=16),
                        xpwT[:, chg * 96 + 80:chg * 96 + 96],
                        x_s[:, chg * TBS + lo:chg * TBS + hi],
                        start=(chg == 0), stop=(chg == 3)),
                        reads=["xpwT", f"x_s{chg}"], writes=pskeys("B", lo, hi))
            op("dve", lambda q: q.tensor_copy(xdbl[0:80, :], psum("A", 0, TBS, rows=80)),
               reads=pskeys("A", 0, TBS), writes=["xdbl"])
            op("dve", lambda q: q.tensor_copy(xdbl_c[0:16, :], psum("B", 0, TBS, rows=16)),
               reads=pskeys("B", 0, TBS), writes=["xdbl_c"])
            dma("sp", f"cxi{tb}", io[f"ccx_in{tb}"][0:80, :], xdbl[0:80, :], reads=["xdbl"])
            dma("sp", f"cxi{tb}", io[f"ccx_in{tb}"][80:96, :], xdbl_c[0:16, :], reads=["xdbl_c"])
            s.buf[f"ccxi{tb}"] = {"w": ("dma", f"cxi{tb}", 2), "r": []}
            coll(lambda q, tb=tb: q.collective_compute(
                "AllReduce", ALU.add, replica_groups=GROUPS,
                ins=[io[f"ccx_in{tb}"].opt()], outs=[io[f"ccx_out{tb}"].opt()]),
                f"arx{tb}", reads=[f"ccxi{tb}"], writes=[f"ccxo{tb}"])
            dma("sp", f"cxo{tb}", xdbl[0:64, :], io[f"ccx_out{tb}"][0:64, :],
                reads=[f"ccxo{tb}"], writes=["xdbl"])
            dma("sp", f"cxo{tb}", xdbl_b[0:16, :], io[f"ccx_out{tb}"][64:80, :],
                reads=[f"ccxo{tb}"], writes=["xdbl_b"])
            dma("sp", f"cxo{tb}", xdbl_c[0:16, :], io[f"ccx_out{tb}"][80:96, :],
                reads=[f"ccxo{tb}"], writes=["xdbl_c"])
            if mstop <= 2:
                break

            for chg in range(4):
                pa = "A" if chg % 2 == 0 else "B"
                for t5 in range(2):
                    lo, hi = t5 * 512, (t5 + 1) * 512
                    op("pe", lambda q, chg=chg, lo=lo, hi=hi, pa=pa: q.matmul(
                        psum(pa, lo, hi), dtwT[0:64, chg * 128:(chg + 1) * 128],
                        xdbl[0:64, lo:hi], start=True, stop=True),
                        reads=["dtwT", "xdbl"], writes=pskeys(pa, lo, hi))
                op("act", lambda q, chg=chg, pa=pa: q.activation(
                    stage[:, 0:TBS], psum(pa, 0, TBS), AF.Relu, bias=dtb[:, chg:chg + 1]),
                    reads=pskeys(pa, 0, TBS) + ["dtb"], writes=["stage"])
                op("act", lambda q, chg=chg, pa=pa: q.activation(
                    stage2[:, 0:TBS], psum(pa, 0, TBS), AF.Relu,
                    bias=ndtb[:, chg:chg + 1], scale=-1.0),
                    reads=pskeys(pa, 0, TBS) + ["ndtb"], writes=["stage2"])
                op("dve", lambda q: q.tensor_tensor(
                    sqM[:, 0:TBS], stage[:, 0:TBS], stage2[:, 0:TBS], ALU.add),
                    reads=["stage", "stage2"], writes=["sqM", "ostage"])
                op("act", lambda q: q.activation(tmpM[:, 0:TBS], sqM[:, 0:TBS], AF.Exp, scale=-1.0),
                   reads=["sqM"], writes=["tmpM", "Brep0"])
                op("act", lambda q: q.activation(sqM[:, 0:TBS], tmpM[:, 0:TBS], AF.Ln, bias=1.0),
                   reads=["tmpM"], writes=["sqM", "ostage"])
                op("dve", lambda q, chg=chg: q.tensor_tensor(
                    delta[:, chg * TBS:(chg + 1) * TBS], stage[:, 0:TBS], sqM[:, 0:TBS], ALU.add),
                    reads=["stage", "sqM"], writes=[f"delta{chg}"])
                op("dve", lambda q, chg=chg: q.tensor_tensor(
                    dx[:, chg * TBS:(chg + 1) * TBS],
                    delta[:, chg * TBS:(chg + 1) * TBS],
                    x_s[:, chg * TBS:(chg + 1) * TBS], ALU.mult),
                    reads=[f"delta{chg}", f"x_s{chg}"], writes=[f"dx{chg}"])
            if mstop <= 3:
                break

            cprev, ccur = carry[gidx % 2], carry[(gidx + 1) % 2]
            kprev, kcur = f"carry{gidx % 2}", f"carry{(gidx + 1) % 2}"
            for chg in range(4):
                op("dve", lambda q, chg=chg: q.tensor_single_scalar(
                    yacc[:, chg * TBS:(chg + 1) * TBS], x_s[:, chg * TBS:(chg + 1) * TBS],
                    dvec[:, chg:chg + 1], ALU.mult),
                    reads=[f"x_s{chg}"], writes=[f"yacc{chg}"], scalar_reads=["dvec"])

            def bc_mms(n):
                for t5 in range(2):
                    lo, hi = t5 * 512, (t5 + 1) * 512
                    op("pe", lambda q, n=n, lo=lo, hi=hi: q.matmul(
                        psum("A", lo, hi), sel16[0:16, n * 128:(n + 1) * 128],
                        xdbl_b[0:16, lo:hi], start=True, stop=True),
                        reads=["sel16", "xdbl_b"], writes=pskeys("A", lo, hi))
                    op("pe", lambda q, n=n, lo=lo, hi=hi: q.matmul(
                        psum("B", lo, hi), sel16[0:16, n * 128:(n + 1) * 128],
                        xdbl_c[0:16, lo:hi], start=True, stop=True),
                        reads=["sel16", "xdbl_c"], writes=pskeys("B", lo, hi))

            def emit_reps(n):
                pn = n % 2
                op("act", lambda q, pn=pn: q.activation(
                    Brep[pn][:, :], psum("A", 0, TBS), AF.Copy),
                    reads=pskeys("A", 0, TBS), writes=[f"Brep{pn}"])
                op("act", lambda q, pn=pn: q.activation(
                    Crep[pn][:, :], psum("B", 0, TBS), AF.Copy),
                    reads=pskeys("B", 0, TBS), writes=[f"Crep{pn}"])

            def emit_a(pr, n):
                pn = n % 2
                for lane in range(2):
                    chg = 2 * pr + lane
                    op("act", lambda q, chg=chg, n=n, lane=lane, pn=pn: q.activation(
                        a_t[lane][pn][:, 1:TBS + 1],
                        delta[:, chg * TBS:(chg + 1) * TBS], AF.Exp,
                        scale=A_sb[:, chg * 16 + n:chg * 16 + n + 1]),
                        reads=[f"delta{chg}"], writes=[f"a_t{lane}_{pn}"],
                        scalar_reads=["A_sb"])

            for pr in range(2):
                for chg in (2 * pr, 2 * pr + 1):
                    pa = "A" if chg % 2 == 0 else "B"
                    for t5 in range(2):
                        lo, hi = t5 * 512, (t5 + 1) * 512
                        op("pe", lambda q, chg=chg, pa=pa, lo=lo, hi=hi: q.matmul(
                            psum(pa, 1024 + lo, 1024 + hi), id128b[:, :],
                            yacc[:, chg * TBS + lo:chg * TBS + hi],
                            start=True, stop=False),
                            reads=[f"yacc{chg}", "id128b"],
                            writes=pskeys(pa, 1024 + lo, 1024 + hi))
                bc_mms(0)
                emit_a(pr, 0)
                emit_reps(0)
                bc_mms(1)
                for n in range(16):
                    pn = n % 2
                    if n < 15:
                        emit_a(pr, n + 1)
                        emit_reps(n + 1)
                    if n < 14:
                        bc_mms(n + 2)
                    for chg in (2 * pr, 2 * pr + 1):
                        col = chg * 16 + n
                        cp_ = chg % 2
                        st, stk = s_t[cp_], f"s_t{cp_}"
                        op("dve", lambda q, col=col, cprev=cprev: q.tensor_copy(
                            b_t[:, 0:1], cprev[:, col:col + 1]),
                            reads=[kprev], writes=["b_t"])
                        op("dve", lambda q, chg=chg, pn=pn: q.tensor_tensor(
                            b_t[:, 1:TBS + 1], dx[:, chg * TBS:(chg + 1) * TBS],
                            Brep[pn][:, :], ALU.mult),
                            reads=[f"dx{chg}", f"Brep{pn}"], writes=["b_t"])
                        init, initreads = cprev[:, col:col + 1], [kprev]
                        op("dve", lambda q, init=init, st=st, cp_=cp_, pn=pn: q.tensor_tensor_scan(
                            st[:, 0:TBS + 1], a_t[cp_][pn][:, 0:TBS + 1], b_t[:, 0:TBS + 1],
                            init, ALU.mult, ALU.add),
                            reads=[f"a_t{cp_}_{pn}", "b_t"], writes=[stk], scalar_reads=initreads)
                        wdst, wkey = (w_t, "w_t") if cp_ == 0 else (w_t2, "w_t2")
                        wk2 = [wkey, "rstd1b"] if wkey == "w_t" else [wkey]
                        op("pool", lambda q, wdst=wdst, st=st, pn=pn: q.tensor_tensor(
                            wdst[:, :], st[:, 1:TBS + 1], Crep[pn][:, :], ALU.mult),
                            reads=[stk, f"Crep{pn}"], writes=wk2)
                        op("pool", lambda q, col=col, ccur=ccur, st=st: q.tensor_copy(
                            ccur[:, col:col + 1], st[:, TBS:TBS + 1]),
                            reads=[stk], writes=[kcur])
                        pa = "A" if cp_ == 0 else "B"
                        for t5 in range(2):
                            lo, hi = t5 * 512, (t5 + 1) * 512
                            op("pe", lambda q, pa=pa, lo=lo, hi=hi, wdst=wdst, n=n: q.matmul(
                                psum(pa, 1024 + lo, 1024 + hi), id128b[:, :],
                                wdst[:, lo:hi], start=False, stop=(n == 15)),
                                reads=[wkey, "id128b"],
                                writes=pskeys(pa, 1024 + lo, 1024 + hi))
                for chg in (2 * pr, 2 * pr + 1):
                    pa = "A" if chg % 2 == 0 else "B"
                    op("dve", lambda q, chg=chg, pa=pa: q.tensor_tensor(
                        yg[:, chg * TBS:(chg + 1) * TBS], psum(pa, 1024, 2048),
                        sz[:, chg * TBS:(chg + 1) * TBS], ALU.mult),
                        reads=pskeys(pa, 1024, 2048) + [f"sz{chg}"], writes=[f"yg{chg}"])
            gidx += 1
            if mstop <= 5:
                break

            for mg in range(8):
                pa = "A" if mg % 2 == 0 else "B"
                for t5 in range(2):
                    lo, hi = t5 * 512, (t5 + 1) * 512
                    for chg in range(4):
                        op("pe", lambda q, mg=mg, chg=chg, lo=lo, hi=hi, pa=pa: q.matmul(
                            psum(pa, lo, hi),
                            woutT[:, chg * 1024 + mg * 128:chg * 1024 + (mg + 1) * 128],
                            yg[:, chg * TBS + lo:chg * TBS + hi],
                            start=(chg == 0), stop=(chg == 3)),
                            reads=["woutT", f"yg{chg}"], writes=pskeys(pa, lo, hi))
                op("dve", lambda q, mg=mg, pa=pa: q.tensor_copy(
                    ostage[:, 0:TBS], psum(pa, 0, TBS)),
                    reads=pskeys(pa, 0, TBS), writes=["ostage", "sqM"])
                dma("sp", "ccm", io["ccm_in"][tb * D + mg * 128:tb * D + (mg + 1) * 128, :],
                    ostage[:, 0:TBS], reads=["ostage"])

        if mstop >= 8:
            s.buf["ccmi"] = {"w": ("dma", "ccm", s.dma_cnt.get("ccm", 0)), "r": []}
            coll(lambda q: q.collective_compute(
                "ReduceScatter", ALU.add, replica_groups=GROUPS,
                ins=[io["ccm_in"].opt()], outs=[io["ccm_out"].opt()]),
                "rsm", reads=["ccmi"], writes=["ccmo"])
        s.barrier()
        AB.pop()
        AFR.pop()

        W2K = 2048

        def stats2048(src, srckey_fn, sqt, sqkey, tag, scrA, scrAk, scrB, scrBk):
            for dg in range(8):
                op("act", lambda q, dg=dg: q.activation(
                    sqt[:, 0:W2K], src[:, dg * W2K:(dg + 1) * W2K], AF.Square),
                    reads=srckey_fn(dg), writes=[sqkey])
                for t5 in range(4):
                    lo, hi = t5 * 512, (t5 + 1) * 512
                    op("pe", lambda q, dg=dg, lo=lo, hi=hi: q.matmul(
                        psum("A", lo, hi, rows=1), onescol_bf[:, 0:1],
                        src[:, dg * W2K + lo:dg * W2K + hi],
                        start=(dg == 0), stop=(dg == 7)),
                        reads=srckey_fn(dg) + ["onescol_bf"], writes=pskeys("A", lo, hi))
                    op("pe", lambda q, dg=dg, lo=lo, hi=hi: q.matmul(
                        psum("B", lo, hi, rows=1), onescol_bf[:, 0:1],
                        sqt[:, lo:hi], start=(dg == 0), stop=(dg == 7)),
                        reads=[sqkey, "onescol_bf"], writes=pskeys("B", lo, hi))
            op("dve", lambda q: q.tensor_single_scalar(
                stat[0:1, 0:W2K], psum("A", 0, W2K, rows=1), 1.0 / D, ALU.mult),
                reads=pskeys("A", 0, W2K), writes=["stat_m" + tag])
            op("dve", lambda q: q.tensor_single_scalar(
                scrA[0:1, 0:W2K], psum("B", 0, W2K, rows=1), 1.0 / D, ALU.mult),
                reads=pskeys("B", 0, W2K), writes=[scrAk])
            op("dve", lambda q: q.tensor_tensor(
                scrB[0:1, 0:W2K], stat[0:1, 0:W2K], stat[0:1, 0:W2K], ALU.mult),
                reads=["stat_m" + tag], writes=[scrBk])
            op("dve", lambda q: q.tensor_tensor(
                scrB[0:1, 0:W2K], scrA[0:1, 0:W2K], scrB[0:1, 0:W2K], ALU.subtract),
                reads=[scrAk, scrBk], writes=[scrBk])
            op("act", lambda q: q.activation(
                scrB[0:1, 0:W2K], scrB[0:1, 0:W2K], AF.Ln, bias=EPS),
                reads=[scrBk, "cpool"], writes=[scrBk])
            op("act", lambda q: q.activation(
                stat[32:33, 0:W2K], scrB[0:1, 0:W2K], AF.Exp, scale=-0.5),
                reads=[scrBk], writes=["stat_r" + tag])
            op("dve", lambda q: q.tensor_copy(rowbf[0:1, 0:W2K], stat[0:1, 0:W2K]),
               reads=["stat_m" + tag], writes=["rowbf_m" + tag])
            op("act", lambda q: q.activation(
                rowbf[32:33, 0:W2K], stat[32:33, 0:W2K], AF.Copy),
                reads=["stat_r" + tag], writes=["rowbf_r" + tag])

        def brow(dst, dstkey, rowpart, rowkey):
            for t5 in range(4):
                lo, hi = t5 * 512, (t5 + 1) * 512
                op("pe", lambda q, lo=lo, hi=hi: q.matmul(
                    psum("B", lo, hi), ones_bf[rowpart:rowpart + 1, 0:128],
                    rowbf[rowpart:rowpart + 1, lo:hi], start=True, stop=True),
                    reads=[rowkey, "ones_bf"], writes=pskeys("B", lo, hi))
            op("dve", lambda q: q.tensor_copy(dst[:, 0:W2K], psum("B", 0, W2K)),
               reads=pskeys("B", 0, W2K), writes=[dstkey])

        if mstop >= 8:
            resid = AB.alloc(16 * 1024)
            AB.push()
            AFR.push()
            mo = AB.alloc(16 * 1024)
            bc2 = AB.alloc(2 * W2K)
            sqR = AB.alloc(W2K)
            hst = [AFR.alloc(W2K), AFR.alloc(W2K)]
            for tbl in range(2):
                dma("sp", "moin",
                    mo.rearrange("p (g tb t) -> p g tb t", g=8, tb=2)[:, :, tbl, :],
                    io["ccm_out"][tbl * 1024:(tbl + 1) * 1024, :]
                        .rearrange("(g p) t -> p g t", p=128),
                    reads=["ccmo"], writes=["mo"])
            stats2048(mo, lambda dg: ["mo"], sqR, "sqR", "2", hst[0], "hrs0", hst[1], "hrs1")
            brow(bc2[:, 0:W2K], "m2b", 0, "rowbf_m2")
            brow(bc2[:, W2K:2 * W2K], "r2b", 32, "rowbf_r2")
            for dg in range(8):
                sl = dg % 2
                dma("sp", f"hr{sl}", hst[sl][:, 0:W2K],
                    io["hres"][dg * 128:(dg + 1) * 128, :], writes=[f"hrs{sl}"])
                op("dve", lambda q, dg=dg: q.tensor_tensor(
                    sqR[:, 0:W2K], mo[:, dg * W2K:(dg + 1) * W2K], bc2[:, 0:W2K], ALU.subtract),
                    reads=["mo", "m2b"], writes=["sqR"])
                op("dve", lambda q: q.tensor_tensor(
                    sqR[:, 0:W2K], sqR[:, 0:W2K], bc2[:, W2K:2 * W2K], ALU.mult),
                    reads=["sqR", "r2b"], writes=["sqR"])
                op("dve", lambda q, dg=dg, sl=sl: q.tensor_tensor(
                    resid[:, dg * W2K:(dg + 1) * W2K], hst[sl][:, 0:W2K], sqR[:, 0:W2K], ALU.add),
                    reads=[f"hrs{sl}", "sqR"], writes=[f"resid{dg}"])
            s.barrier()
            AB.pop()
            AFR.pop()

        if mstop >= 9:
            AB.push()
            AFR.push()
            wslot = [AB.alloc(8 * 1024) for _ in range(3)]
            h_sc = AB.alloc(8 * 1024)
            acc = AB.alloc(8 * W2K)
            cwsc = AB.alloc(W2K)
            sqF = cwsc
            bcF = AB.alloc(W2K)
            m4b = wslot[0][:, 0:W2K]
            r4b = wslot[0][:, W2K:2 * W2K]
            lg_em = AFR.alloc(W2K)
            smx = AFR.alloc(64)
            finstage = AFR.alloc(W2K)
            outst = AFR.alloc(128)

            def load_w(k):
                sl = k % 3
                dma("sp", f"ws{sl}", wslot[sl].rearrange("p (g h) -> p g h", g=8),
                    io["wbf"][k, :, :].rearrange("(g p) h -> p g h", p=128),
                    reads=["wbf"], writes=[f"wslot{sl}"])

            stats2048(resid, lambda dg: [f"resid{dg}"], sqF, "cwsc", "3", lg_em, "lg_em", finstage, "finstage")
            op("act", lambda q: q.activation(
                rowbf[0:1, 2048:2048 + W2K], finstage[0:1, 0:W2K], AF.Exp, scale=0.5),
                reads=["finstage"], writes=["rowbf_s3"])
            brow(bcF[:, 0:W2K], "r3b", 32, "rowbf_r3")

            for g in range(8):
                op("pe", lambda q, g=g: q.matmul(
                    psum("A", 0, E, rows=1), onescol_bf[:, 0:1],
                    rw_sb[:, g * E:(g + 1) * E],
                    start=(g == 0), stop=(g == 7)),
                    reads=["rw_sb", "onescol_bf"], writes=pskeys("A", 0, 512))
            op("dve", lambda q: q.tensor_single_scalar(
                wrow[0:1, 1024:1024 + E], psum("A", 0, E, rows=1), -1.0, ALU.mult),
                reads=pskeys("A", 0, 512), writes=["w1r"])
            for tc in range(4):
                lo, hi = tc * 512, (tc + 1) * 512
                for g in range(8):
                    op("pe", lambda q, g=g, tc=tc: q.matmul(
                        psum("A", tc * 512, (tc + 1) * 512, rows=15),
                        rw_sb[:, g * E:(g + 1) * E],
                        resid[:, g * W2K + tc * 512:g * W2K + (tc + 1) * 512],
                        start=(g == 0), stop=False),
                        reads=["rw_sb", f"resid{g}"], writes=pskeys("A", tc * 512, (tc + 1) * 512))
                op("pe", lambda q, tc=tc: q.matmul(
                    psum("A", tc * 512, (tc + 1) * 512, rows=15),
                    wrow[0:1, 1024:1024 + E],
                    rowbf[0:1, tc * 512:(tc + 1) * 512],
                    start=False, stop=False),
                    reads=["w1r", "rowbf_m3"], writes=pskeys("A", tc * 512, (tc + 1) * 512))
                op("pe", lambda q, tc=tc: q.matmul(
                    psum("A", tc * 512, (tc + 1) * 512, rows=15),
                    rbrow[0:1, 0:E],
                    rowbf[0:1, 2048 + tc * 512:2048 + (tc + 1) * 512],
                    start=False, stop=True),
                    reads=["rbrow", "rowbf_s3"], writes=pskeys("A", tc * 512, (tc + 1) * 512))
            op("dve", lambda q: q.tensor_tensor(
                lg_em[0:15, 0:W2K], psum("A", 0, W2K, rows=15), bcF[0:15, 0:W2K], ALU.mult),
                reads=pskeys("A", 0, W2K) + ["r3b"], writes=["lg_em"])

            for tt in range(16):
                op("pe", lambda q, tt=tt: q.transpose(
                    psum("B", 0, 15),
                    lg_em[0:15, tt * 128:(tt + 1) * 128], id128[0:15, 0:15]),
                    reads=["lg_em", "id128"], writes=pskeys("B", 0, 128))
                op("dve", lambda q: q.tensor_copy(smx[:, 0:15], psum("B", 0, 15)),
                   reads=pskeys("B", 0, 128), writes=["lgt"])
                op("dve", lambda q: q.reduce_max(smx[:, 15:16], smx[:, 0:15], AX.X, negate=True),
                   reads=["lgt"], writes=["negm"])
                op("act", lambda q: q.activation(smx[:, 16:31], smx[:, 0:15], AF.Exp,
                                                 bias=smx[:, 15:16]),
                   reads=["lgt"], writes=["ex"], scalar_reads=["negm"])
                op("dve", lambda q: q.reduce_sum(smx[:, 31:32], smx[:, 16:31], AX.X),
                   reads=["ex"], writes=["ssum"])
                op("dve", lambda q: q.reciprocal(smx[:, 32:33], smx[:, 31:32]),
                   reads=["ssum"], writes=["rec"])
                op("dve", lambda q: q.tensor_single_scalar(
                    smx[:, 0:15], smx[:, 16:31], smx[:, 32:33], ALU.mult),
                    reads=["ex"], writes=["probs"], scalar_reads=["rec"])
                op("dve", lambda q: q.reduce_max(smx[:, 15:16], smx[:, 0:15], AX.X),
                   reads=["probs"], writes=["m1"])
                op("dve", lambda q: q.tensor_single_scalar(
                    smx[:, 16:31], smx[:, 0:15], smx[:, 15:16], ALU.is_ge),
                    reads=["probs"], writes=["g1"], scalar_reads=["m1"])
                op("dve", lambda q: q.tensor_scalar(
                    smx[:, 16:31], smx[:, 16:31], -1.0, 1.0, ALU.mult, ALU.add),
                    reads=["g1"], writes=["g1"])
                op("dve", lambda q: q.tensor_tensor(
                    smx[:, 16:31], smx[:, 0:15], smx[:, 16:31], ALU.mult),
                    reads=["probs", "g1"], writes=["p1"])
                op("dve", lambda q: q.reduce_max(smx[:, 15:16], smx[:, 16:31], AX.X),
                   reads=["p1"], writes=["m2"])
                op("dve", lambda q: q.tensor_single_scalar(
                    smx[:, 33:48], smx[:, 16:31], smx[:, 15:16], ALU.is_ge),
                    reads=["p1"], writes=["g2"], scalar_reads=["m2"])
                op("dve", lambda q: q.tensor_scalar(
                    smx[:, 33:48], smx[:, 33:48], -1.0, 1.0, ALU.mult, ALU.add),
                    reads=["g2"], writes=["g2"])
                op("dve", lambda q: q.tensor_tensor(
                    smx[:, 33:48], smx[:, 16:31], smx[:, 33:48], ALU.mult),
                    reads=["p1", "g2"], writes=["p2"])
                op("dve", lambda q: q.reduce_max(smx[:, 15:16], smx[:, 33:48], AX.X),
                   reads=["p2"], writes=["m3"])
                op("dve", lambda q: q.tensor_single_scalar(
                    smx[:, 33:48], smx[:, 0:15], smx[:, 15:16], ALU.is_ge),
                    reads=["probs"], writes=["gk"], scalar_reads=["m3"])
                op("dve", lambda q: q.tensor_tensor(
                    smx[:, 0:15], smx[:, 0:15], smx[:, 33:48], ALU.mult),
                    reads=["probs", "gk"], writes=["cw"])
                op("pe", lambda q, tt=tt: q.transpose(
                    psum("B", 512, 640, rows=15),
                    smx[:, 0:15], id128[:, :]),
                    reads=["cw", "id128"], writes=pskeys("B", 512, 1024))
                op("act", lambda q, tt=tt: q.activation(
                    cwT_sb[0:15, tt * 128:(tt + 1) * 128], psum("B", 512, 640, rows=15),
                    AF.Copy),
                    reads=pskeys("B", 512, 1024), writes=["cwT"])

            for k in range(2):
                load_w(k)
            import os as _os
            _nli = int(_os.environ.get("NLI", "16"))
            for li in (range(_nli) if mstop > 9 else []):
                s1, s2 = (2 * li) % 3, (2 * li + 1) % 3
                slb = li % 2
                if 2 * li + 2 < 2 * _nli:
                    load_w(2 * li + 2)
                if li > 0:
                    dma("sp", f"eb{slb}", ebst[slb][0:1, 0:H],
                        io["ebbf"][:, (li - 1) * H:li * H],
                        reads=["ebbf"], writes=[f"ebr{slb}"])
                    dma("sp", f"eb{slb}", ebst[slb][0:1, H:2 * H],
                        io["ebbf"][:, E * H + (li - 1) * D:E * H + li * D],
                        reads=["ebbf"], writes=[f"ebr{slb}"])
                for t5 in range(2):
                    lo, hi = t5 * 512, (t5 + 1) * 512
                    for g in range(8):
                        op("pe", lambda q, g=g, lo=lo, hi=hi, s1=s1: q.matmul(
                            psum("B", 1024 + lo, 1024 + hi, rows=1), onescol_bf[:, 0:1],
                            wslot[s1][:, g * 1024 + lo:g * 1024 + hi],
                            start=(g == 0), stop=(g == 7)),
                            reads=[f"wslot{s1}", "onescol_bf"],
                            writes=pskeys("B", 1024 + lo, 1024 + hi))
                op("dve", lambda q: q.tensor_single_scalar(
                    wrow[0:1, 0:1024], psum("B", 1024, 2048, rows=1), -1.0, ALU.mult),
                    reads=pskeys("B", 1024, 2048), writes=["w1e"])
                if li > 0:
                    e = li - 1
                    for tc in range(4):
                        op("pe", lambda q, e=e, tc=tc: q.matmul(
                            psum("A", tc * 512, (tc + 1) * 512),
                            sel16[0:15, e * 128:(e + 1) * 128],
                            cwT_sb[0:15, tc * 512:(tc + 1) * 512],
                            start=True, stop=True),
                            reads=["sel16", "cwT"], writes=pskeys("A", tc * 512, (tc + 1) * 512))
                    op("act", lambda q: q.activation(
                        cwrow[0:1, 0:W2K], psum("A", 0, W2K, rows=1), AF.Copy),
                        reads=pskeys("A", 0, W2K), writes=["cwrow"])
                    op("dve", lambda q: q.tensor_tensor(
                        cwsc[:, 0:W2K], psum("A", 0, W2K), bcF[:, 0:W2K], ALU.mult),
                        reads=pskeys("A", 0, W2K) + ["r3b", "cwrow"], writes=["cwsc"])
                for tch in range(2):
                    toff = tch * 1024
                    for hc in range(8):
                        pa = "A" if hc % 2 == 0 else "B"
                        for tc in range(2):
                            lo, hi = tc * 512, (tc + 1) * 512
                            for g in range(8):
                                op("pe", lambda q, g=g, hc=hc, lo=lo, hi=hi, pa=pa,
                                   s1=s1, toff=toff: q.matmul(
                                    psum(pa, lo, hi),
                                    wslot[s1][:, g * 1024 + hc * 128:g * 1024 + (hc + 1) * 128],
                                    resid[:, g * W2K + toff + lo:g * W2K + toff + hi],
                                    start=(g == 0), stop=False),
                                    reads=[f"wslot{s1}", f"resid{g}"], writes=pskeys(pa, lo, hi))
                            op("pe", lambda q, hc=hc, lo=lo, hi=hi, pa=pa, toff=toff: q.matmul(
                                psum(pa, lo, hi), wrow[0:1, hc * 128:(hc + 1) * 128],
                                rowbf[0:1, toff + lo:toff + hi], start=False, stop=False),
                                reads=["w1e", "rowbf_m3"], writes=pskeys(pa, lo, hi))
                            brow_ap = (sb1row[0:1, hc * 128:(hc + 1) * 128] if li == 0
                                       else ebst[slb][0:1, hc * 128:(hc + 1) * 128])
                            brow_key = "sb1row" if li == 0 else f"ebr{slb}"
                            op("pe", lambda q, brow_ap=brow_ap, lo=lo, hi=hi, pa=pa,
                               toff=toff: q.matmul(
                                psum(pa, lo, hi), brow_ap,
                                rowbf[0:1, 2048 + toff + lo:2048 + toff + hi], start=False, stop=True),
                                reads=[brow_key, "rowbf_s3"], writes=pskeys(pa, lo, hi))
                        op("act", lambda q, hc=hc, pa=pa: q.activation(
                            h_sc[:, hc * 1024:(hc + 1) * 1024], psum(pa, 0, 1024), AF.Relu),
                            reads=pskeys(pa, 0, 1024), writes=[f"hsc{hc}"])
                        sc_key = "cwsc" if li > 0 else "r3b"
                        sc_tile = cwsc if li > 0 else bcF
                        op("dve", lambda q, hc=hc, sc_tile=sc_tile, toff=toff: q.tensor_tensor(
                            h_sc[:, hc * 1024:(hc + 1) * 1024], h_sc[:, hc * 1024:(hc + 1) * 1024],
                            sc_tile[:, toff:toff + 1024], ALU.mult),
                            reads=[f"hsc{hc}", sc_key], writes=[f"hsc{hc}"])
                    for dt in range(8):
                        pa = "A" if dt % 2 == 0 else "B"
                        for tc in range(2):
                            lo, hi = tc * 512, (tc + 1) * 512
                            for hc in range(8):
                                op("pe", lambda q, hc=hc, dt=dt, lo=lo, hi=hi, pa=pa,
                                   s2=s2: q.matmul(
                                    psum(pa, lo, hi),
                                    wslot[s2][:, hc * 1024 + dt * 128:hc * 1024 + (dt + 1) * 128],
                                    h_sc[:, hc * 1024 + lo:hc * 1024 + hi],
                                    start=(hc == 0), stop=False),
                                    reads=[f"wslot{s2}", f"hsc{hc}"], writes=pskeys(pa, lo, hi))
                            if li == 0:
                                op("pe", lambda q, dt=dt, lo=lo, hi=hi, pa=pa: q.matmul(
                                    psum(pa, lo, hi),
                                    sb2row[0:1, dt * 128:(dt + 1) * 128],
                                    ones_bf[0:1, 0:512], start=False, stop=True),
                                    reads=["sb2row", "ones_bf"], writes=pskeys(pa, lo, hi))
                            else:
                                op("pe", lambda q, dt=dt, lo=lo, hi=hi, pa=pa, slb=slb,
                                   toff=toff: q.matmul(
                                    psum(pa, lo, hi),
                                    ebst[slb][0:1, H + dt * 128:H + (dt + 1) * 128],
                                    cwrow[0:1, toff + lo:toff + hi], start=False, stop=True),
                                    reads=[f"ebr{slb}", "cwrow"], writes=pskeys(pa, lo, hi))
                        if li == 0:
                            op("dve", lambda q, dt=dt, pa=pa, toff=toff: q.tensor_copy(
                                acc[:, dt * W2K + toff:dt * W2K + toff + 1024],
                                psum(pa, 0, 1024)),
                                reads=pskeys(pa, 0, 1024), writes=[f"acc{dt}"])
                        else:
                            op("dve", lambda q, dt=dt, pa=pa, toff=toff: q.tensor_tensor(
                                acc[:, dt * W2K + toff:dt * W2K + toff + 1024],
                                acc[:, dt * W2K + toff:dt * W2K + toff + 1024],
                                psum(pa, 0, 1024), ALU.add),
                                reads=[f"acc{dt}"] + pskeys(pa, 0, 1024), writes=[f"acc{dt}"])
                if 2 * li + 3 < 2 * _nli and tch == 1 and dt == 7:
                    load_w(2 * li + 3)


            do_final = mstop > 10
            if do_final:
                stats2048(acc, lambda dg: [f"acc{dg}"], sqF, "cwsc", "4", lg_em, "lg_em", finstage, "finstage")
                brow(m4b, "wslot0", 0, "rowbf_m4")
            
                brow(r4b, "wslot0", 32, "rowbf_r4")
                for dg in range(8):
                    op("dve", lambda q, dg=dg: q.tensor_tensor(
                        sqF[:, 0:W2K], acc[:, dg * W2K:(dg + 1) * W2K],
                        m4b, ALU.subtract),
                        reads=[f"acc{dg}", "wslot0"], writes=["sqFn"])
                    op("dve", lambda q: q.tensor_tensor(
                        sqF[:, 0:W2K], sqF[:, 0:W2K], r4b, ALU.mult),
                        reads=["sqFn", "wslot0"], writes=["sqFn"])
                    op("dve", lambda q, dg=dg: q.tensor_tensor(
                        finstage[:, 0:W2K], resid[:, dg * W2K:(dg + 1) * W2K],
                        sqF[:, 0:W2K], ALU.add),
                        reads=[f"resid{dg}", "sqFn"], writes=["finstage"])
                    for tt in range(16):
                        po = (tt % 4) * 512
                        op("pe", lambda q, tt=tt, po=po: q.transpose(
                            psum("B", po, po + 128), finstage[:, tt * 128:(tt + 1) * 128],
                            id128[:, :]),
                            reads=["finstage", "id128"], writes=pskeys("B", po, po + 128))
                        op("dve", lambda q, po=po: q.tensor_copy(
                            outst[:, 0:128], psum("B", po, po + 128)),
                            reads=pskeys("B", po, po + 128), writes=["outst"])
                        dma("sp", "out", io["out"][tt * 128:(tt + 1) * 128, dg * 128:(dg + 1) * 128],
                            outst[:, 0:128], reads=["outst"])

        if dbg_names:
            s.barrier()
            named = {
                "hbf": hbf, "x_s": x_s, "sz": sz, "delta": delta, "dx": dx,
                "yg": yg, "xdbl": xdbl, "xdbl_c": xdbl_c,
                "carry0": carry[0][:, :], "carry1": carry[1][:, :],
                "yacc": yacc,
                "s_t": s_t[0].bitcast(BF16), "b_t": b_t, "a_t2": a_t[0].bitcast(BF16),
            }
            if mstop >= 8:
                named["resid"] = resid
                named["mo"] = mo
            if mstop >= 9:
                named["cwT"] = cwT_sb[0:15, :]
                named["lg_em"] = lg_em[0:15, 0:2048]
                named["cwsc"] = cwsc[:, 0:2048]
                named["hsc"] = h_sc
                named["ws0"] = wslot[0]
                named["ws2"] = wslot[2]
            if mstop >= 9:
                named["acc"] = acc
            for i, nm in enumerate(dbg_names):
                src = named[nm]
                p, cols = src.shape[0], src.shape[1]
                dma("pool", "dbg", dbg[i][0:p, 0:cols], src[0:p, 0:cols])
        s.barrier()
    with nc.Block() as block:
        s.replay(block)
    es.close()
    return nc, io




def kernel(**inputs):
    from concourse.bass_utils import run_bass_kernel_spmd
    maps = host_prep(inputs)
    nc, io = build_program("full")
    res = run_bass_kernel_spmd(nc, maps, core_ids=list(range(NCORE)))
    outs = []
    for bb in range(B):
        rows = [np.asarray(res.results[4 * bb + q]["out"]) for q in range(4)]
        outs.append(np.concatenate(rows, axis=0))
    return np.stack(outs).astype(np.float32)
